# version 11 (final)
# baseline (speedup 1.0000x reference)
import math
from contextlib import ExitStack

import numpy as np
import ml_dtypes
import concourse.bass as bass
import concourse.mybir as mybir
from concourse.bass_utils import run_bass_kernel_spmd

F32 = mybir.dt.float32
BF16 = mybir.dt.bfloat16
AF = mybir.ActivationFunctionType
ALU = mybir.AluOpType
AX = mybir.AxisListType

S = 4096
D = 2048
NH = 16
HD = 128
KC = 16
NE = 8
ROT = 32
EPS = 1e-5
ISQ = 1.0 / math.sqrt(128.0)
BIGL = 1024.0

PHASES = ("p0", "p1", "p2", "p3", "p4", "p5")
DEBUG = ()


class Sem:
    def __init__(self, h):
        self.h = h
        self.n = 0


class Emitter:
    ENGS = ("sync", "gpsimd", "tensor", "scalar", "vector")

    def __init__(self, nc):
        self.nc = nc
        self.q = {k: [] for k in self.ENGS}

    def wait(self, eng, sem, val):
        assert val <= sem.n, f"wait on {val} but only {sem.n} scheduled ({eng})"
        if val <= 0:
            return
        self.q[eng].append(lambda e, h=sem.h, v=val: e.wait_ge(h, v))

    def op(self, eng, fn, sem=None, k=1):
        if sem is not None:
            sem.n += k
            self.q[eng].append(lambda e, f=fn, h=sem.h, kk=k: f(e).then_inc(h, kk))
        else:
            self.q[eng].append(lambda e, f=fn: f(e))

    def flush(self, name):
        with self.nc.Block(name) as blk:
            for eng in self.ENGS:
                fns = self.q[eng]
                if fns:
                    getattr(blk, eng)(lambda e, t=tuple(fns): [f(e) for f in t])
        self.q = {k: [] for k in self.ENGS}


def dap(t, offset, dims):
    return bass.AP(t, offset, [list(d) for d in dims])


def build_program(debug=()):
    nc = bass.Bass("TRN2", target_bir_lowering=False)

    def scratch(name, shape, dtype):
        kind = "ExternalOutput" if name in debug else "Internal"
        return nc.dram_tensor(name, shape, dtype, kind=kind)

    x_in = nc.dram_tensor("x_in", [S, D], F32, kind="ExternalInput")
    wqkvT = nc.dram_tensor("wqkvT", [D, 3 * D], F32, kind="ExternalInput")
    woutT = nc.dram_tensor("woutT", [D, D], F32, kind="ExternalInput")
    wr_in = nc.dram_tensor("wr_in", [D, 7], F32, kind="ExternalInput")
    wup_in = nc.dram_tensor("wup_in", [NE, D, D], F32, kind="ExternalInput")
    wgate_in = nc.dram_tensor("wgate_in", [NE, D, D], F32, kind="ExternalInput")
    wdown_in = nc.dram_tensor("wdown_in", [NE, D, D], F32, kind="ExternalInput")
    bup_in = nc.dram_tensor("bup_in", [KC, 128, NE], F32, kind="ExternalInput")
    bgate_in = nc.dram_tensor("bgate_in", [KC, 128, NE], F32, kind="ExternalInput")
    bdmat_in = nc.dram_tensor("bdmat_in", [NE, D], F32, kind="ExternalInput")
    costab_in = nc.dram_tensor("costab_in", [128, 32 * ROT], F32, kind="ExternalInput")
    sintab_in = nc.dram_tensor("sintab_in", [128, 32 * ROT], F32, kind="ExternalInput")
    br_in = nc.dram_tensor("br_in", [128, 7], F32, kind="ExternalInput")
    gb_in = nc.dram_tensor("gb_in", [128, 7], F32, kind="ExternalInput")
    identb_in = nc.dram_tensor("identb_in", [128, 128], BF16, kind="ExternalInput")
    identf_in = nc.dram_tensor("identf_in", [128, 128], F32, kind="ExternalInput")
    onesb_in = nc.dram_tensor("onesb_in", [128, 128], BF16, kind="ExternalInput")
    onesf_in = nc.dram_tensor("onesf_in", [1, 128], F32, kind="ExternalInput")
    masks_in = nc.dram_tensor("masks_in", [2 * 128, 1024], BF16, kind="ExternalInput")

    q_hbm = scratch("q_hbm", [D, S], BF16)
    k_hbm = scratch("k_hbm", [D, S], BF16)
    v_hbm = scratch("v_hbm", [S, D], BF16)
    attn_hbm = scratch("attn_hbm", [D, S], BF16)
    x2_hbm = scratch("x2_hbm", [S, D], F32)
    xg_hbm = scratch("xg_hbm", [D, S], BF16)
    u_hbm = scratch("u_hbm", [NE * D, S], F32)
    g_hbm = scratch("g_hbm", [NE * D, S], BF16)
    prow_hbm = scratch("prow_hbm", [NE, S], BF16)
    r_hbm = scratch("r_hbm", [32, 128], F32)
    probs8f_hbm = scratch("probs8f_hbm", [NE, S], F32)
    x_out = nc.dram_tensor("x_out", [S, D], F32, kind="ExternalOutput")

    st = ExitStack()

    def sbt(name, shape, dtype, stack=None):
        return (stack or st).enter_context(nc.sbuf_tensor(name, shape, dtype))

    def psumt(name, shape, dtype, stack=None):
        return (stack or st).enter_context(nc.psum_tensor(name, shape, dtype))

    with st:
        constf_t = sbt("constf", [128, 560], F32)
        constb_t = sbt("constb", [128, 384], BF16)
        bup_sb = constf_t[:, 0:128]
        bgate_sb = constf_t[:, 128:256]
        br_bc = constf_t[:, 256:263]
        gb_bc = constf_t[:, 263:270]
        epsb = constf_t[:, 270:271]
        mxx = constf_t[:, 272:304]
        identf = constf_t[:, 304:432]
        onesf = constf_t[0:1, 432:560]
        identb = constb_t[:, 0:128]
        onesb = constb_t[:, 128:256]
        wr_sb = constb_t[:, 256:368]

        sems = {n: Sem(st.enter_context(nc.semaphore(n))) for n in
                ("sL", "sSt", "sW", "sPE", "sPE2", "sACT", "sACT2", "sDVE", "sDVE2", "sX")}
        sL, sSt, sW = sems["sL"], sems["sSt"], sems["sW"]
        sPE, sPE2 = sems["sPE"], sems["sPE2"]
        sACT, sACT2 = sems["sACT"], sems["sACT2"]
        sDVE, sDVE2 = sems["sDVE"], sems["sDVE2"]
        sX = sems["sX"]

        em = Emitter(nc)
        xmark = {}

        for dst, src in ((identb, identb_in), (identf, identf_in),
                         (onesb, onesb_in), (onesf, onesf_in),
                         (br_bc, br_in), (gb_bc, gb_in)):
            em.op("sync", lambda e, d=dst, s=src: e.dma_start(d[:], s.ap()), sL, 16)
        em.op("sync", lambda e: e.dma_start(
            bup_sb[:].rearrange("p (c n) -> p c n", c=KC),
            bup_in.ap().rearrange("c p n -> p c n")), sL, 16)
        em.op("sync", lambda e: e.dma_start(
            bgate_sb[:].rearrange("p (c n) -> p c n", c=KC),
            bgate_in.ap().rearrange("c p n -> p c n")), sL, 16)
        em.op("gpsimd", lambda e: e.dma_start(
            wr_sb[:].rearrange("p (c n) -> p c n", c=KC),
            wr_in.ap().rearrange("(c p) n -> p c n", p=128)), sW, 16)
        em.op("vector", lambda e: e.memset(epsb[:], EPS), sDVE, 1)
        em.op("vector", lambda e: e.memset(mxx[:], 0.0), sDVE, 1)
        em.flush("setup")

        if "p0" in PHASES:
            p0 = ExitStack()
            with p0:
                costab = sbt("costab", [128, 32 * ROT], F32, p0)
                sintab = sbt("sintab", [128, 32 * ROT], F32, p0)
                xln_fm = sbt("xln_fm", [128, KC * (S // 2)], BF16, p0)
                x_t = [sbt(f"p0x{i}", [128, D], F32, p0) for i in range(2)]
                stt = [sbt(f"p0s{i}", [128, 16], F32, p0) for i in range(2)]
                xln_tm = [sbt(f"p0n{i}", [128, D], BF16, p0) for i in range(2)]
                sq_scr = sbt("p0sq", [128, D], F32, p0)
                rope_t = sbt("p0rt", [128, 512], F32, p0)
                wbf = [sbt(f"p0w{i}", [128, KC * 512], BF16, p0) for i in range(2)]
                scr_f = [sbt(f"p0f{i}", [128, 512], F32, p0) for i in range(3)]
                tm_bf = [sbt(f"p0b{i}", [128, 512], BF16, p0) for i in range(3)]
                qk_stage = sbt("p0qs", [128, 4 * (S // 2)], BF16, p0)
                sqdump = sbt("p0sd", [128, 512], F32, p0)
                sqtmp = [sbt(f"p0st{i}", [128, 4], F32, p0) for i in range(2)]
                rr_all = sbt("p0rr", [128, 32], F32, p0)
                v_bf = [sbt(f"p0v{i}", [128, 512], BF16, p0) for i in range(2)]
                psA = [psumt(f"p0pA{i}", [128, 512], F32, p0) for i in range(2)]
                ptb = [psumt(f"p0pT{i}", [128, 1024], BF16, p0) for i in range(2)]

                em.op("sync", lambda e: e.dma_start(costab[:], costab_in.ap()), sL, 16)
                em.op("sync", lambda e: e.dma_start(sintab[:], sintab_in.ap()), sL, 16)
                tab_v = sL.n

                def c(b, i):
                    return stt[b][:, i:i + 1]

                act_mark = {}
                x_sq_mark = {}
                r_mark = {}
                pe_mark = {}
                pt_free = {0: (None, 0), 1: (None, 0)}
                psA_free = {0: (None, 0), 1: (None, 0)}
                stage_free = [0, 0, 0, 0]
                vst_free = [0, 0]
                ropetm_free = [(None, 0), (None, 0), (None, 0)]
                sqtmp_free = [(None, 0), (None, 0)]
                mm_after_half = []
                mm_slab_mark = {}
                stage_free_wc = 0
                sq_mark = {}

                for half in range(2):
                    base_t = half * 16
                    for ti in range(16):
                        tg = base_t + ti
                        b = ti % 2
                        if tg >= 2:
                            em.wait("sync", sDVE2, act_mark[tg - 2])
                            em.wait("sync", sACT, x_sq_mark[tg - 2])
                        em.op("sync", lambda e, b=b, tg=tg: e.dma_start(
                            x_t[b][:], x_in.ap()[tg * 128:(tg + 1) * 128, :]), sL, 16)
                        ldv = sL.n
                        em.wait("vector", sL, ldv)
                        em.op("vector", lambda e, b=b: e.reduce_sum(
                            c(b, 0), x_t[b][:], AX.X), sDVE2, 1)
                        em.wait("vector", sDVE2, sDVE2.n)
                        em.op("vector", lambda e, b=b: e.tensor_scalar(
                            c(b, 2), c(b, 0), 1.0 / D, None, ALU.mult), sDVE2, 1)
                        em.wait("vector", sDVE2, sDVE2.n)
                        if tg >= 2:
                            em.wait("vector", sPE, pe_mark[tg - 2])
                        em.op("vector", lambda e, b=b: e.tensor_scalar(
                            xln_tm[b][:], x_t[b][:], c(b, 2), None,
                            ALU.subtract), sDVE2, 1)
                        act_mark[tg] = sDVE2.n
                        em.wait("scalar", sL, ldv)
                        if tg >= 1:
                            em.wait("scalar", sDVE2, sq_mark[tg - 1])
                        em.op("scalar", lambda e, b=b: e.activation(
                            sq_scr[:], x_t[b][:], AF.Square), sACT, 1)
                        x_sq_mark[tg] = sACT.n
                        em.wait("vector", sACT, sACT.n)
                        em.op("vector", lambda e, b=b: e.reduce_sum(
                            c(b, 1), sq_scr[:], AX.X), sDVE2, 1)
                        sq_mark[tg] = sDVE2.n
                        em.wait("vector", sDVE2, sDVE2.n)
                        em.op("vector", lambda e, b=b: e.tensor_scalar(
                            c(b, 4), c(b, 1), 1.0 / D, None, ALU.mult), sDVE2, 1)
                        em.op("vector", lambda e, b=b: e.tensor_tensor(
                            c(b, 3), c(b, 2), c(b, 2), ALU.mult), sDVE2, 1)
                        em.wait("vector", sDVE2, sDVE2.n)
                        em.op("vector", lambda e, b=b: e.tensor_sub(
                            c(b, 5), c(b, 4), c(b, 3)), sDVE2, 1)
                        var_v = sDVE2.n
                        em.wait("scalar", sDVE2, var_v)
                        em.op("scalar", lambda e, b=b: e.activation(
                            c(b, 6), c(b, 5), AF.Sqrt, bias=epsb[:]), sACT, 1)
                        em.wait("vector", sACT, sACT.n)
                        em.op("vector", lambda e, b=b, tg=tg: e.reciprocal(
                            rr_all[:, tg:tg + 1], c(b, 6)), sDVE2, 1)
                        r_mark[tg] = sDVE2.n
                        em.wait("tensor", sDVE2, act_mark[tg])
                        if half == 1 and ti == 0:
                            em.wait("vector", sPE2, mm_after_half[0])
                        for kb in range(4):
                            pb = kb % 2
                            ws, wv = pt_free[pb]
                            if ws is not None:
                                em.wait("tensor", ws, wv)
                            for j in range(4):
                                kc = kb * 4 + j
                                em.op("tensor", lambda e, b=b, kc=kc, j=j, pb=pb: e.transpose(
                                    ptb[pb][:, j * 128:(j + 1) * 128],
                                    xln_tm[b][:, kc * 128:(kc + 1) * 128],
                                    identb[:]), sPE if j == 3 else None, 1)
                            em.wait("vector", sPE, sPE.n)
                            em.op("vector", lambda e, kb=kb, ti=ti, pb=pb: e.tensor_copy(
                                dap(xln_fm, kb * 4 * 2048 + ti * 128,
                                    [[KC * 2048, 128], [2048, 4], [1, 128]]),
                                ptb[pb][:, 0:512]), sDVE, 1)
                            pt_free[pb] = (sDVE, sDVE.n)
                        pe_mark[tg] = sPE.n
                    xln_ready = sDVE.n

                    pend_tr = []

                    def emit_tr(ib, ti, rope_v, wc, tg):
                        em.wait("tensor", sDVE2, rope_v)
                        pb = ti % 2
                        ws, wv = pt_free[pb]
                        if ws is not None:
                            em.wait("tensor", ws, wv)
                        for hh in range(4):
                            em.op("tensor", lambda e, pb=pb, ib=ib, hh=hh: e.transpose(
                                ptb[pb][:, hh * 128:(hh + 1) * 128],
                                tm_bf[ib][:, hh * 128:(hh + 1) * 128],
                                identb[:]), sPE if hh == 3 else None, 1)
                        em.wait("vector", sPE, sPE.n)
                        if ti == 0:
                            em.wait("vector", sSt, stage_free_wc)
                        em.op("vector", lambda e, ti=ti, pb=pb: e.tensor_copy(
                            dap(qk_stage, ti * 128, [[4 * 2048, 128], [2048, 4], [1, 128]]),
                            ptb[pb][:, 0:512]), sDVE, 1)
                        pt_free[pb] = (sDVE, sDVE.n)
                        ropetm_free[ib] = (sDVE, sDVE.n)
                        em.wait("scalar", sDVE2, rope_v)
                        sb2 = ti % 2
                        ws, wv = sqtmp_free[sb2]
                        if ws is not None:
                            em.wait("scalar", ws, wv)
                        for hh in range(4):
                            em.op("scalar", lambda e, ib=ib, hh=hh, sb2=sb2: e.activation(
                                sqdump[:, hh * 128:(hh + 1) * 128],
                                tm_bf[ib][:, hh * 128:(hh + 1) * 128], AF.Square,
                                accum_out=sqtmp[sb2][:, hh:hh + 1]), sACT, 1)
                        base = (0 if wc < 4 else 16) + (wc % 4) * 4
                        em.wait("vector", sACT, sACT.n)
                        em.op("vector", lambda e, base=base, sb2=sb2: e.tensor_tensor(
                            mxx[:, base:base + 4], mxx[:, base:base + 4],
                            sqtmp[sb2][:], ALU.max), sDVE2, 1)
                        sqtmp_free[sb2] = (sDVE2, sDVE2.n)

                    for wc in range(12):
                        wbuf = (half * 12 + wc) % 2
                        nslab = half * 12 + wc
                        if nslab >= 2:
                            em.wait("gpsimd", sPE2, mm_slab_mark[nslab - 2])
                        em.op("gpsimd", lambda e, wbuf=wbuf, wc=wc: e.dma_start(
                            wbf[wbuf][:].rearrange("p (c n) -> p c n", c=KC),
                            wqkvT.ap().rearrange("(c p) n -> p c n", p=128)
                            [:, :, wc * 512:(wc + 1) * 512]), sW, 16)
                        w_v = sW.n
                        is_v = wc >= 8
                        for ti in range(16):
                            tg = base_t + ti
                            idx = nslab * 16 + ti
                            ib = idx % 2
                            ib3 = idx % 3
                            if ti == 0:
                                em.wait("tensor", sW, w_v)
                                em.wait("tensor", sDVE, xln_ready)
                            ws, wv = psA_free[ib]
                            if ws is not None:
                                em.wait("tensor", ws, wv)
                            for kc in range(KC):
                                em.op("tensor", lambda e, ib=ib, kc=kc, ti=ti, wbuf=wbuf: e.matmul(
                                    psA[ib][:],
                                    xln_fm[:, kc * 2048 + ti * 128: kc * 2048 + (ti + 1) * 128],
                                    wbf[wbuf][:, kc * 512:(kc + 1) * 512],
                                    start=(kc == 0), stop=(kc == KC - 1)),
                                    sPE2 if kc == KC - 1 else None, 1)
                            mm_v = sPE2.n
                            if is_v:
                                vc = wc - 8
                                em.wait("scalar", sPE2, mm_v)
                                em.wait("scalar", sSt, vst_free[ib])
                                em.wait("scalar", sDVE2, r_mark[tg])
                                em.op("scalar", lambda e, ib=ib, tg=tg: e.activation(
                                    v_bf[ib][:], psA[ib][:], AF.Copy,
                                    scale=rr_all[:, tg:tg + 1]), sACT2, 1)
                                psA_free[ib] = (sACT2, sACT2.n)
                                em.wait("sync", sACT2, sACT2.n)
                                em.op("sync", lambda e, ib=ib, tg=tg, vc=vc: e.dma_start(
                                    v_hbm.ap()[tg * 128:(tg + 1) * 128, vc * 512:(vc + 1) * 512],
                                    v_bf[ib][:]), sSt, 16)
                                vst_free[ib] = sSt.n
                            else:
                                em.wait("scalar", sPE2, mm_v)
                                ws, wv = ropetm_free[ib3]
                                if ws is not None:
                                    em.wait("scalar", ws, wv)
                                em.wait("scalar", sDVE2, r_mark[tg])
                                em.op("scalar", lambda e, ib=ib, ib3=ib3, tg=tg: e.activation(
                                    scr_f[ib3][:], psA[ib][:], AF.Copy,
                                    scale=rr_all[:, tg:tg + 1]), sACT2, 1)
                                em.op("scalar", lambda e, ib=ib, ib3=ib3, tg=tg: e.activation(
                                    tm_bf[ib3][:], psA[ib][:], AF.Copy,
                                    scale=rr_all[:, tg:tg + 1]), sACT2, 1)
                                cp_v = sACT2.n
                                psA_free[ib] = (sACT2, cp_v)
                                em.wait("vector", sACT2, cp_v)
                                if wc == 0 and ti == 0:
                                    em.wait("vector", sL, tab_v)

                                def r3(t, lo):
                                    return t[:].rearrange("p (h cc) -> p h cc", h=4)[:, :, lo:lo + 32]

                                ctb = dap(costab, tg * ROT, [[32 * ROT, 128], [0, 4], [1, ROT]])
                                stb = dap(sintab, tg * ROT, [[32 * ROT, 128], [0, 4], [1, ROT]])
                                tv = [dap(rope_t, 128 * i, [[512, 128], [32, 4], [1, ROT]])
                                      for i in range(4)]
                                x1 = r3(scr_f[ib3], 0)
                                x2 = r3(scr_f[ib3], 32)
                                em.op("vector", lambda e, x1=x1, c_=ctb, o=tv[0]: e.tensor_tensor(o, x1, c_, ALU.mult))
                                em.op("vector", lambda e, x2=x2, s_=stb, o=tv[1]: e.tensor_tensor(o, x2, s_, ALU.mult), sDVE2, 1)
                                em.op("vector", lambda e, x2=x2, c_=ctb, o=tv[2]: e.tensor_tensor(o, x2, c_, ALU.mult))
                                em.op("vector", lambda e, x1=x1, s_=stb, o=tv[3]: e.tensor_tensor(o, x1, s_, ALU.mult), sDVE2, 1)
                                y1 = r3(tm_bf[ib3], 0)
                                y2 = r3(tm_bf[ib3], 32)
                                em.wait("vector", sDVE2, sDVE2.n)
                                em.op("vector", lambda e, o=y1, a=tv[0], b2=tv[1]: e.tensor_tensor(o, a, b2, ALU.subtract))
                                em.op("vector", lambda e, o=y2, a=tv[2], b2=tv[3]: e.tensor_tensor(o, a, b2, ALU.add), sDVE2, 1)
                                rope_v = sDVE2.n
                                pend_tr.append((ib3, ti, rope_v, wc, tg))
                                if len(pend_tr) > 2:
                                    emit_tr(*pend_tr.pop(0))
                        while pend_tr:
                            emit_tr(*pend_tr.pop(0))
                        mm_slab_mark[nslab] = sPE2.n
                        if not is_v:
                            dstt = q_hbm if wc < 4 else k_hbm
                            for hh in range(4):
                                h = (wc % 4) * 4 + hh
                                em.wait("sync", sDVE, sDVE.n)
                                em.op("sync", lambda e, hh=hh, h=h, dstt=dstt, half=half: e.dma_start(
                                    dstt.ap()[h * 128:(h + 1) * 128,
                                              half * 2048:(half + 1) * 2048],
                                    qk_stage[:, hh * 2048:(hh + 1) * 2048]), sSt, 16)
                            stage_free_wc = sSt.n
                    mm_after_half.append(sPE2.n)
                xmark["sq_done"] = sDVE2.n
                em.flush("p0")

        if "p1" in PHASES:
            p1 = ExitStack()
            with p1:
                masks = sbt("masks", [128, 2 * 1024], BF16, p1)
                qf = [sbt(f"p1q{i}", [128, S], BF16, p1) for i in range(2)]
                kf = [sbt(f"p1k{i}", [128, S], BF16, p1) for i in range(2)]
                vsb = [sbt(f"p1v{i}", [128, S], BF16, p1) for i in range(2)]
                mx2 = sbt("p1m2", [128, 4], F32, p1)
                crow = sbt("p1cr", [1, 16], F32, p1)
                cneg_all = sbt("p1cn", [128, 16], F32, p1)
                pT = [sbt(f"p1p{i}", [128, 1024], BF16, p1) for i in range(3)]
                l_row = sbt("p1l", [1, 512], F32, p1)
                lbc = sbt("p1lb", [128, 512], F32, p1)
                lbs = sbt("p1lbs", [128, 512], F32, p1)
                ast = [sbt(f"p1a{i}", [128, S], BF16, p1) for i in range(2)]
                tsb = [sbt(f"p1ts{i}", [128, 512], BF16, p1) for i in range(5)]
                psS2 = [psumt(f"p1pS{i}", [128, 1024], F32, p1) for i in range(2)]
                psO = [psumt(f"p1pO{i}", [128, 512], F32, p1) for i in range(2)]
                psL = psumt("p1pL", [128, 512], F32, p1)
                psB = psumt("p1pB", [128, 512], F32, p1)

                em.op("sync", lambda e: e.dma_start(
                    masks[:].rearrange("p (j n) -> p j n", j=2),
                    masks_in.ap().rearrange("(j p) n -> p j n", p=128)), sL, 16)
                masks_v = sL.n

                def head_load(h, guard):
                    hb = h % 2
                    if guard is not None:
                        em.wait("sync", sPE2, guard)
                    em.op("sync", lambda e, hb=hb, h=h: e.dma_start(
                        qf[hb][:], q_hbm.ap()[h * 128:(h + 1) * 128, :]), sL, 16)
                    em.op("sync", lambda e, hb=hb, h=h: e.dma_start(
                        kf[hb][:], k_hbm.ap()[h * 128:(h + 1) * 128, :]), sL, 16)
                    em.op("sync", lambda e, hb=hb, h=h: e.dma_start(
                        vsb[hb][:], dap(v_hbm, h * 128, [[D, 128], [128 * D, 32], [1, 128]])),
                        sL, 16)
                    return sL.n

                h_load = [head_load(0, None), head_load(1, None)]
                ast_free = [0, 0]
                lbc_free = (None, 0)
                pT_free = [(None, 0), (None, 0), (None, 0)]
                tsb_free = [(None, 0)] * 5
                psS_free = [(None, 0), (None, 0)]
                psO_free = [(None, 0), (None, 0)]
                psL_free = [(None, 0)]
                first_mask = [True]

                em.op("vector", lambda e: e.memset(mx2[:], 0.0), sDVE, 1)
                em.wait("tensor", sDVE, sDVE.n)
                em.wait("tensor", sDVE2, xmark["sq_done"])
                em.op("tensor", lambda e: e.transpose(
                    psB[0:16, 0:128], mxx[:, 0:16], identf[:]))
                em.op("tensor", lambda e: e.transpose(
                    psB[0:16, 128:256], mxx[:, 16:32], identf[:]), sPE, 1)
                em.wait("vector", sPE, sPE.n)
                em.op("vector", lambda e: e.tensor_reduce(
                    mx2[0:16, 0:1], psB[0:16, 0:128], AX.X, ALU.max))
                em.op("vector", lambda e: e.tensor_reduce(
                    mx2[0:16, 1:2], psB[0:16, 128:256], AX.X, ALU.max), sDVE, 1)
                em.wait("vector", sDVE, sDVE.n)
                em.op("vector", lambda e: e.tensor_tensor(
                    mx2[0:16, 2:3], mx2[0:16, 0:1], mx2[0:16, 1:2], ALU.add), sDVE, 1)
                em.wait("vector", sDVE, sDVE.n)
                em.op("vector", lambda e: e.tensor_scalar(
                    mx2[0:16, 3:4], mx2[0:16, 2:3], -0.5 * ISQ, None, ALU.mult), sDVE, 1)
                em.wait("tensor", sDVE, sDVE.n)
                em.op("tensor", lambda e: e.transpose(
                    psB[0:1, 256:384], mx2[:, 3:4], identf[:]), sPE, 1)
                em.wait("vector", sPE, sPE.n)
                em.op("vector", lambda e: e.tensor_copy(crow[:], psB[0:1, 256:272]), sDVE, 1)
                em.wait("tensor", sDVE, sDVE.n)
                em.op("tensor", lambda e: e.matmul(
                    psB[:, 384:400], onesf[:], crow[:], start=True, stop=True), sPE, 1)
                em.wait("scalar", sPE, sPE.n)
                em.op("scalar", lambda e: e.copy(cneg_all[:], psB[:, 384:400]), sACT2, 1)
                cneg_v = sACT2.n
                psB_free = (sACT2, cneg_v)

                for h in range(NH):
                    hb = h % 2
                    for qg in range(8):
                        npair = 2 * qg + 2
                        ob = qg % 2
                        ws, wv = psO_free[ob]
                        if ws is not None:
                            em.wait("tensor", ws, wv)

                        def emit_pv(pr, pb, ki0, gate, npair=npair, ob=ob, hb=hb):
                            em.wait("tensor", gate[0], gate[1])
                            for half in range(2):
                                last = (pr == npair - 1) and (half == 1)
                                em.op("tensor", lambda e, ob=ob, hb=hb, pb=pb, ki0=ki0, half=half, pr=pr, last=last: e.matmul(
                                    psO[ob][:],
                                    vsb[hb][:, (ki0 + half) * 128:(ki0 + half + 1) * 128],
                                    pT[pb][:, half * 512:(half + 1) * 512],
                                    start=(pr == 0 and half == 0), stop=last),
                                    sPE2 if half == 1 else None, 1)
                            pT_free[pb] = (sPE2, sPE2.n)

                        def emit_l(pb3, ts_v, first, lastp):
                            if first:
                                ws, wv = psL_free[0]
                                if ws is not None:
                                    em.wait("tensor", ws, wv)
                            em.wait("tensor", sDVE2, ts_v)
                            em.op("tensor", lambda e, pb3=pb3, first=first, lastp=lastp: e.matmul(
                                psL[0:1, :], onesb[:, 0:1], tsb[pb3][:],
                                start=first, stop=lastp), sPE2, 1)
                            tsb_free[pb3] = (sPE2, sPE2.n)

                        pend = []
                        pendL = []
                        for pr in range(npair):
                            ki0 = 2 * pr
                            pb = pr % 3
                            pb2 = pr % 2
                            pb3 = pr % 5
                            if qg == 0 and pr == 0:
                                em.wait("tensor", sL, h_load[h])
                                if h == 0:
                                    em.wait("scalar", sACT2, cneg_v)
                            ws, wv = psS_free[pb2]
                            if ws is not None:
                                em.wait("tensor", ws, wv)
                            for half in range(2):
                                em.op("tensor", lambda e, pb=pb2, hb=hb, ki0=ki0, qg=qg, half=half: e.matmul(
                                    psS2[pb][:, half * 512:(half + 1) * 512],
                                    kf[hb][:, (ki0 + half) * 128:(ki0 + half + 1) * 128],
                                    qf[hb][:, qg * 512:(qg + 1) * 512],
                                    start=True, stop=True),
                                    sPE if half == 1 else None, 1)
                            smm_v = sPE.n
                            em.wait("scalar", sPE, smm_v)
                            ws, wv = pT_free[pb]
                            if ws is not None:
                                em.wait("scalar", ws, wv)
                            em.op("scalar", lambda e, pb=pb, pb2=pb2, h=h: e.activation(
                                pT[pb][:], psS2[pb2][:], AF.Exp, bias=cneg_all[:, h:h + 1],
                                scale=ISQ), sACT, 1)
                            exp_v = sACT.n
                            psS_free[pb2] = (sACT, exp_v)
                            jp = pr - (npair - 2)
                            if jp >= 0:
                                if first_mask[0]:
                                    em.wait("vector", sL, masks_v)
                                    first_mask[0] = False
                                em.wait("vector", sACT, exp_v)
                                em.op("vector", lambda e, pb=pb, jp=jp: e.tensor_tensor(
                                    pT[pb][:], pT[pb][:], masks[:, jp * 1024:(jp + 1) * 1024],
                                    ALU.mult), sDVE2, 1)
                                gate = (sDVE2, sDVE2.n)
                            else:
                                gate = (sACT, exp_v)
                                em.wait("vector", sACT, exp_v)
                            ws, wv = tsb_free[pb3]
                            if ws is not None:
                                em.wait("vector", ws, wv)
                            em.op("vector", lambda e, pb=pb, pb3=pb3: e.tensor_tensor(
                                tsb[pb3][:], pT[pb][:, 0:512], pT[pb][:, 512:1024],
                                ALU.add), sDVE2, 1)
                            pendL.append((pb3, sDVE2.n, pr == 0, pr == npair - 1))
                            pend.append((pr, pb, ki0, gate))
                            if len(pend) > 2:
                                emit_pv(*pend.pop(0))
                            if len(pendL) > 4:
                                emit_l(*pendL.pop(0))
                        while pend:
                            emit_pv(*pend.pop(0))
                        while pendL:
                            emit_l(*pendL.pop(0))
                        pv_done = sPE2.n
                        em.wait("vector", sPE2, pv_done)
                        em.op("vector", lambda e: e.tensor_copy(l_row[:], psL[0:1, :]), sDVE, 1)
                        psL_free[0] = (sDVE, sDVE.n)
                        em.wait("tensor", sDVE, sDVE.n)
                        ws, wv = psB_free
                        if ws is not None:
                            em.wait("tensor", ws, wv)
                        em.op("tensor", lambda e: e.matmul(
                            psB[:], onesf[:], l_row[:], start=True, stop=True), sPE, 1)
                        em.wait("scalar", sPE, sPE.n)
                        ws, wv = lbc_free
                        if ws is not None:
                            em.wait("scalar", ws, wv)
                        em.op("scalar", lambda e: e.copy(lbc[:], psB[:]), sACT2, 1)
                        psB_free = (sACT2, sACT2.n)
                        em.wait("vector", sACT2, sACT2.n)
                        em.op("vector", lambda e: e.reciprocal(lbs[:], lbc[:]), sDVE2, 1)
                        em.wait("vector", sDVE2, sDVE2.n)
                        if qg == 0:
                            em.wait("vector", sSt, ast_free[hb])
                        em.op("vector", lambda e, ob=ob, hb=hb, qg=qg: e.tensor_tensor(
                            ast[hb][:, qg * 512:(qg + 1) * 512], psO[ob][:], lbs[:],
                            ALU.mult), sDVE2, 1)
                        lbc_free = (sDVE2, sDVE2.n)
                        psO_free[ob] = (sDVE2, sDVE2.n)
                    em.wait("sync", sDVE2, sDVE2.n)
                    em.op("sync", lambda e, hb=hb, h=h: e.dma_start(
                        attn_hbm.ap()[h * 128:(h + 1) * 128, :], ast[hb][:]), sSt, 16)
                    ast_free[hb] = sSt.n
                    if h + 2 < NH:
                        h_load.append(head_load(h + 2, sPE2.n))
                em.flush("p1")

        if "p2" in PHASES:
            p2 = ExitStack()
            with p2:
                wout_bf = sbt("p2w", [128, KC * D], BF16, p2)
                probs8 = sbt("probs8", [8, S], F32, p2)
                asl = [sbt(f"p2a{i}", [128, KC * 512], BF16, p2) for i in range(2)]
                x_t = [sbt(f"p2x{i}", [128, D], F32, p2) for i in range(2)]
                x2_t = [sbt(f"p2y{i}", [128, D], F32, p2) for i in range(2)]
                stt = [sbt(f"p2s{i}", [128, 16], F32, p2) for i in range(2)]
                sq_scr = sbt("p2sq", [128, D], F32, p2)
                xg_tm = [sbt(f"p2n{i}", [128, D], BF16, p2) for i in range(2)]
                xg_st = [sbt(f"p2g{i}", [128, KC * 512], BF16, p2) for i in range(2)]
                g8 = [sbt(f"p2r{i}", [128, 48], F32, p2) for i in range(2)]
                r_coll = sbt("p2rc", [128, 32], F32, p2)
                rT = sbt("p2rt", [32, 128], F32, p2)
                ppad = [sbt(f"p2pp{i}", [128, 128], F32, p2) for i in range(2)]
                psW = [psumt(f"p2pW{i}", [128, 512], F32, p2) for i in range(2)]
                ptb = [psumt(f"p2pT{i}", [128, 1024], BF16, p2) for i in range(2)]
                psR = psumt("p2pR", [128, 512], F32, p2)
                ptf = psumt("p2pF", [128, 512], F32, p2)

                em.op("gpsimd", lambda e: e.dma_start(
                    wout_bf[:].rearrange("p (c n) -> p c n", c=KC),
                    woutT.ap().rearrange("(c p) n -> p c n", p=128)), sW, 16)
                wout_v = sW.n
                for i in range(2):
                    em.op("vector", lambda e, i=i: e.memset(ppad[i][:], 0.0))
                    em.op("vector", lambda e, i=i: e.memset(ppad[i][:, 7:8], 1.0), sDVE, 1)

                def rcol(b, i, w=7):
                    return g8[b][:, i:i + w]

                asl_free = [0, 0]
                xt_free = [0, 0]
                sq_mark = {}
                x2w_free = [0, 0]
                x2n_free = [0, 0]
                x2sq_free = [0, 0]
                r_mark2 = {}
                xg_free = [0, 0]
                psW_free = {0: (None, 0), 1: (None, 0)}
                pt_free = {0: (None, 0), 1: (None, 0)}
                ppad_free = [(None, 0), (None, 0)]
                psR_free = (None, 0)
                ptf_free = (None, 0)
                asl_v_tab = {}

                def load_asl(sl):
                    sb_ = sl % 2
                    em.wait("sync", sDVE, asl_free[sb_])
                    em.op("sync", lambda e, sb_=sb_, sl=sl: e.dma_start(
                        asl[sb_][:].rearrange("p (c n) -> p c n", c=KC),
                        dap(attn_hbm, sl * 512, [[S, 128], [128 * S, KC], [1, 512]])), sL, 16)
                    asl_v_tab[sl] = sL.n

                load_asl(0)
                for sl in range(8):
                    sb_ = sl % 2
                    gb_ = sl % 2
                    asl_v = asl_v_tab[sl]
                    for t4 in range(4):
                        ti = sl * 4 + t4
                        tb = ti % 2
                        em.wait("sync", sDVE, xt_free[tb])
                        em.op("sync", lambda e, tb=tb, ti=ti: e.dma_start(
                            x_t[tb][:], x_in.ap()[ti * 128:(ti + 1) * 128, :]), sL, 16)
                        x_v = sL.n
                        if t4 == 0 and sl + 1 < 8:
                            load_asl(sl + 1)
                        for oc in range(4):
                            pb = oc % 2
                            if t4 == 0 and oc == 0:
                                em.wait("tensor", sL, asl_v)
                                if sl == 0:
                                    em.wait("tensor", sW, wout_v)
                            ws, wv = psW_free[pb]
                            if ws is not None:
                                em.wait("tensor", ws, wv)
                            for kc in range(KC):
                                em.op("tensor", lambda e, pb=pb, sb_=sb_, kc=kc, t4=t4, oc=oc: e.matmul(
                                    psW[pb][:],
                                    asl[sb_][:, kc * 512 + t4 * 128: kc * 512 + (t4 + 1) * 128],
                                    wout_bf[:, kc * D + oc * 512: kc * D + oc * 512 + 512],
                                    start=(kc == 0), stop=(kc == KC - 1)),
                                    sPE if kc == KC - 1 else None, 1)
                            em.wait("vector", sPE, sPE.n)
                            em.wait("vector", sL, x_v)
                            if oc == 0:
                                em.wait("vector", sSt, x2w_free[tb])
                                em.wait("vector", sDVE2, x2n_free[tb])
                                em.wait("vector", sACT, x2sq_free[tb])
                            em.op("vector", lambda e, tb=tb, pb=pb, oc=oc: e.tensor_tensor(
                                x2_t[tb][:, oc * 512:(oc + 1) * 512], psW[pb][:],
                                x_t[tb][:, oc * 512:(oc + 1) * 512], ALU.add), sDVE, 1)
                            psW_free[pb] = (sDVE, sDVE.n)
                        x2_v = sDVE.n
                        xt_free[tb] = x2_v
                        if t4 == 3:
                            asl_free[sb_] = x2_v
                        em.wait("sync", sDVE, x2_v)
                        em.op("sync", lambda e, tb=tb, ti=ti: e.dma_start(
                            x2_hbm.ap()[ti * 128:(ti + 1) * 128, :], x2_t[tb][:]), sSt, 16)
                        x2w_free[tb] = sSt.n
                        em.wait("vector", sDVE, x2_v)
                        em.op("vector", lambda e, tb=tb: e.reduce_sum(
                            stt[tb][:, 0:1], x2_t[tb][:], AX.X), sDVE2, 1)
                        em.wait("vector", sDVE2, sDVE2.n)
                        em.op("vector", lambda e, tb=tb: e.tensor_scalar(
                            stt[tb][:, 2:3], stt[tb][:, 0:1], 1.0 / D, None, ALU.mult), sDVE2, 1)
                        em.wait("vector", sDVE2, sDVE2.n)
                        em.op("vector", lambda e, tb=tb: e.tensor_scalar(
                            xg_tm[tb][:], x2_t[tb][:], stt[tb][:, 2:3], None,
                            ALU.subtract), sDVE2, 1)
                        x2n_free[tb] = sDVE2.n
                        em.wait("scalar", sDVE, x2_v)
                        if ti >= 1:
                            em.wait("scalar", sDVE2, sq_mark[ti - 1])
                        em.op("scalar", lambda e, tb=tb: e.activation(
                            sq_scr[:], x2_t[tb][:], AF.Square), sACT, 1)
                        x2sq_free[tb] = sACT.n
                        em.wait("vector", sACT, sACT.n)
                        em.op("vector", lambda e, tb=tb: e.reduce_sum(
                            stt[tb][:, 1:2], sq_scr[:], AX.X), sDVE2, 1)
                        sq_mark[ti] = sDVE2.n
                        em.wait("vector", sDVE2, sDVE2.n)
                        em.op("vector", lambda e, tb=tb: e.tensor_scalar(
                            stt[tb][:, 4:5], stt[tb][:, 1:2], 1.0 / D, None, ALU.mult), sDVE2, 1)
                        em.op("vector", lambda e, tb=tb: e.tensor_tensor(
                            stt[tb][:, 3:4], stt[tb][:, 2:3], stt[tb][:, 2:3], ALU.mult), sDVE2, 1)
                        em.wait("vector", sDVE2, sDVE2.n)
                        em.op("vector", lambda e, tb=tb: e.tensor_sub(
                            stt[tb][:, 5:6], stt[tb][:, 4:5], stt[tb][:, 3:4]), sDVE2, 1)
                        em.wait("scalar", sDVE2, sDVE2.n)
                        em.op("scalar", lambda e, tb=tb: e.activation(
                            stt[tb][:, 6:7], stt[tb][:, 5:6], AF.Sqrt, bias=epsb[:]), sACT, 1)
                        em.wait("vector", sACT, sACT.n)
                        em.op("vector", lambda e, tb=tb, ti=ti: e.reciprocal(
                            r_coll[:, ti:ti + 1], stt[tb][:, 6:7]), sDVE2, 1)
                        r_mark2[ti] = sDVE2.n
                        em.wait("tensor", sDVE2, x2n_free[tb])
                        if t4 == 0:
                            em.wait("tensor", sSt, xg_free[gb_])
                        for kb in range(4):
                            pb = kb % 2
                            ws, wv = pt_free[pb]
                            if ws is not None:
                                em.wait("tensor", ws, wv)
                            for j in range(4):
                                kc = kb * 4 + j
                                em.op("tensor", lambda e, pb=pb, tb=tb, kc=kc, j=j: e.transpose(
                                    ptb[pb][:, j * 128:(j + 1) * 128],
                                    xg_tm[tb][:, kc * 128:(kc + 1) * 128],
                                    identb[:]), sPE if j == 3 else None, 1)
                            em.wait("vector", sPE, sPE.n)
                            em.op("vector", lambda e, pb=pb, sb_=gb_, kb=kb, t4=t4: e.tensor_copy(
                                dap(xg_st[sb_], kb * 4 * 512 + t4 * 128,
                                    [[KC * 512, 128], [512, 4], [1, 128]]),
                                ptb[pb][:, 0:512]), sDVE, 1)
                            pt_free[pb] = (sDVE, sDVE.n)
                        tr_v = sDVE.n
                        em.wait("tensor", sDVE, tr_v)
                        if psR_free[0] is not None:
                            em.wait("tensor", psR_free[0], psR_free[1])
                        for kc in range(KC):
                            em.op("tensor", lambda e, sb_=gb_, kc=kc, t4=t4: e.matmul(
                                psR[:, 0:7],
                                xg_st[sb_][:, kc * 512 + t4 * 128: kc * 512 + (t4 + 1) * 128],
                                wr_sb[:, kc * 7:(kc + 1) * 7],
                                start=(kc == 0), stop=(kc == KC - 1)),
                                sPE if kc == KC - 1 else None, 1)
                        gmm_v = sPE.n
                        tb2 = ti % 2
                        em.wait("vector", sPE, gmm_v)
                        em.wait("vector", sDVE2, r_mark2[ti])
                        em.op("vector", lambda e, tb2=tb2, ti=ti: e.tensor_scalar(
                            rcol(tb2, 0), psR[:, 0:7], r_coll[:, ti:ti + 1], None,
                            ALU.mult), sDVE, 1)
                        psR_free = (sDVE, sDVE.n)
                        em.wait("vector", sDVE, sDVE.n)
                        em.op("vector", lambda e, tb2=tb2: e.tensor_tensor(
                            rcol(tb2, 0), rcol(tb2, 0), br_bc[:], ALU.add), sDVE, 1)
                        em.wait("vector", sDVE, sDVE.n)
                        em.op("vector", lambda e, tb2=tb2: e.tensor_tensor(
                            rcol(tb2, 8), rcol(tb2, 0), gb_bc[:], ALU.add), sDVE, 1)
                        em.wait("vector", sDVE, sDVE.n)
                        em.op("vector", lambda e, tb2=tb2: e.tensor_reduce(
                            g8[tb2][:, 16:17], rcol(tb2, 8), AX.X, ALU.max), sDVE, 1)
                        em.wait("vector", sDVE, sDVE.n)
                        em.op("vector", lambda e, tb2=tb2: e.tensor_scalar(
                            rcol(tb2, 24), rcol(tb2, 8), g8[tb2][:, 16:17], -BIGL,
                            ALU.is_equal, ALU.mult), sDVE, 1)
                        em.wait("vector", sDVE, sDVE.n)
                        em.op("vector", lambda e, tb2=tb2: e.tensor_tensor(
                            rcol(tb2, 24), rcol(tb2, 24), rcol(tb2, 8), ALU.add), sDVE, 1)
                        em.wait("vector", sDVE, sDVE.n)
                        em.op("vector", lambda e, tb2=tb2: e.tensor_reduce(
                            g8[tb2][:, 17:18], rcol(tb2, 24), AX.X, ALU.max), sDVE, 1)
                        em.wait("vector", sDVE, sDVE.n)
                        em.op("vector", lambda e, tb2=tb2: e.tensor_scalar(
                            rcol(tb2, 32), rcol(tb2, 8), g8[tb2][:, 17:18], BIGL,
                            ALU.is_ge, ALU.mult), sDVE, 1)
                        em.wait("vector", sDVE, sDVE.n)
                        em.op("vector", lambda e, tb2=tb2: e.tensor_tensor(
                            rcol(tb2, 32), rcol(tb2, 32), rcol(tb2, 0), ALU.add), sDVE, 1)
                        em.wait("vector", sDVE, sDVE.n)
                        em.op("vector", lambda e, tb2=tb2: e.tensor_reduce(
                            g8[tb2][:, 18:19], rcol(tb2, 32), AX.X, ALU.max,
                            negate=True), sDVE, 1)
                        em.wait("scalar", sDVE, sDVE.n)
                        em.op("scalar", lambda e, tb2=tb2: e.activation(
                            rcol(tb2, 40), rcol(tb2, 32), AF.Exp,
                            bias=g8[tb2][:, 18:19]), sACT2, 1)
                        em.wait("vector", sACT2, sACT2.n)
                        em.op("vector", lambda e, tb2=tb2: e.tensor_reduce(
                            g8[tb2][:, 19:20], rcol(tb2, 40), AX.X, ALU.add), sDVE, 1)
                        em.wait("vector", sDVE, sDVE.n)
                        em.op("vector", lambda e, tb2=tb2: e.reciprocal(
                            g8[tb2][:, 20:21], g8[tb2][:, 19:20]), sDVE, 1)
                        em.wait("vector", sDVE, sDVE.n)
                        ws, wv = ppad_free[tb2]
                        if ws is not None:
                            em.wait("vector", ws, wv)
                        em.op("vector", lambda e, tb2=tb2: e.tensor_scalar(
                            ppad[tb2][:, 0:7], rcol(tb2, 40), g8[tb2][:, 20:21], None,
                            ALU.mult), sDVE, 1)
                        pr_v = sDVE.n
                        em.wait("tensor", sDVE, pr_v)
                        if ptf_free[0] is not None:
                            em.wait("tensor", ptf_free[0], ptf_free[1])
                        em.op("tensor", lambda e, tb2=tb2: e.transpose(
                            ptf[:, 0:128], ppad[tb2][:], identf[:]), sPE, 1)
                        ppad_free[tb2] = (sPE, sPE.n)
                        em.wait("vector", sPE, sPE.n)
                        em.op("vector", lambda e, ti=ti: e.tensor_copy(
                            probs8[0:8, ti * 128:(ti + 1) * 128], ptf[0:8, 0:128]), sDVE, 1)
                        ptf_free = (sDVE, sDVE.n)
                    em.wait("sync", sDVE, sDVE.n)
                    em.op("sync", lambda e, sb_=gb_, sl=sl: e.dma_start(
                        dap(xg_hbm, sl * 512, [[S, 128], [128 * S, KC], [1, 512]]),
                        xg_st[sb_][:].rearrange("p (c n) -> p c n", c=KC)), sSt, 16)
                    xg_free[gb_] = sSt.n
                em.wait("tensor", sDVE2, r_mark2[31])
                if ptf_free[0] is not None:
                    em.wait("tensor", ptf_free[0], ptf_free[1])
                em.op("tensor", lambda e: e.transpose(
                    ptf[0:32, 0:128], r_coll[:], identf[:]), sPE, 1)
                em.wait("vector", sPE, sPE.n)
                em.op("vector", lambda e: e.tensor_copy(rT[:], ptf[0:32, 0:128]), sDVE, 1)
                em.wait("sync", sDVE, sDVE.n)
                em.op("sync", lambda e: e.dma_start(r_hbm.ap(), rT[:]), sSt, 16)
                em.wait("gpsimd", sDVE, sDVE.n)
                em.op("gpsimd", lambda e: e.dma_start(prow_hbm.ap(), probs8[:]), sW, 16)
                xmark["prow_w"] = sW.n
                em.wait("sync", sDVE, sDVE.n)
                em.op("sync", lambda e: e.dma_start(probs8f_hbm.ap(), probs8[:]), sSt, 16)
                em.wait("sync", sSt, sSt.n)
                em.wait("gpsimd", sW, sW.n)
                em.flush("p2")

        if "p3" in PHASES:
            p3 = ExitStack()
            with p3:
                xg_fm = sbt("p3xg", [128, KC * S], BF16, p3)
                wsl = [sbt(f"p3w{i}", [128, KC * 256], BF16, p3) for i in range(2)]
                ust = [sbt(f"p3u{i}", [128, 4 * 512], F32, p3) for i in range(2)]
                r_row = sbt("p3rr", [1, S], F32, p3)
                r_bc = sbt("p3rb", [128, S], F32, p3)
                psU = [psumt(f"p3p{i}", [128, 512], F32, p3) for i in range(2)]

                em.op("sync", lambda e: e.dma_start(
                    r_row[:], dap(r_hbm, 0, [[S, 1], [1, S]])), sL, 16)
                rr_v = sL.n
                em.op("sync", lambda e: e.dma_start(
                    xg_fm[:].rearrange("p (c n) -> p c n", c=KC),
                    dap(xg_hbm, 0, [[S, 128], [128 * S, KC], [1, S]])), sL, 16)
                xg_v = sL.n
                em.wait("tensor", sL, rr_v)
                cp_tab = {}
                for j in range(8):
                    if j >= 2:
                        em.wait("tensor", sDVE, cp_tab[j - 2])
                    em.op("tensor", lambda e, j=j: e.matmul(
                        psU[j % 2][:], onesf[:], r_row[0:1, j * 512:(j + 1) * 512],
                        start=True, stop=True), sPE, 1)
                    em.wait("vector", sPE, sPE.n)
                    em.op("vector", lambda e, j=j: e.tensor_copy(
                        r_bc[:, j * 512:(j + 1) * 512], psU[j % 2][:]), sDVE, 1)
                    cp_tab[j] = sDVE.n
                em.wait("tensor", sDVE, cp_tab[7])
                psU_free = {0: (None, 0), 1: (None, 0)}
                ust_free = [0, 0]
                wslab_mark = {}
                nslab = 0
                for e_ in range(NE):
                    for uc2 in range(8):
                        wb = nslab % 2
                        if nslab >= 2:
                            em.wait("gpsimd", sPE2, wslab_mark[nslab - 2])
                        em.op("gpsimd", lambda e, e_=e_, uc2=uc2, wb=wb: e.dma_start(
                            wsl[wb][:].rearrange("p (c n) -> p c n", c=KC),
                            dap(wup_in, e_ * D * D + uc2 * 256,
                                [[D, 128], [128 * D, KC], [1, 256]])), sW, 16)
                        w_v = sW.n
                        for uch in range(2):
                            uc = uc2 * 2 + uch
                            for tw in range(8):
                                pb = tw % 2
                                ub = (nslab * 4 + uch * 2 + tw // 4) % 2
                                if tw % 4 == 0:
                                    em.wait("vector", sSt, ust_free[ub])
                                if tw == 0:
                                    em.wait("tensor", sW, w_v)
                                    if nslab == 0 and uch == 0:
                                        em.wait("tensor", sL, xg_v)
                                ws, wv = psU_free[pb]
                                if ws is not None:
                                    em.wait("tensor", ws, wv)
                                for kc in range(KC):
                                    em.op("tensor", lambda e, pb=pb, wb=wb, kc=kc, uch=uch, tw=tw: e.matmul(
                                        psU[pb][:],
                                        wsl[wb][:, kc * 256 + uch * 128: kc * 256 + (uch + 1) * 128],
                                        xg_fm[:, kc * S + tw * 512: kc * S + tw * 512 + 512],
                                        start=(kc == 0), stop=(kc == KC - 1)),
                                        sPE2 if kc == KC - 1 else None, 1)
                                em.wait("vector", sPE2, sPE2.n)
                                em.op("vector", lambda e, ub=ub, tw=tw % 4, gtw=tw, pb=pb: e.tensor_tensor(
                                    ust[ub][:, tw * 512:(tw + 1) * 512], psU[pb][:],
                                    r_bc[:, gtw * 512:(gtw + 1) * 512], ALU.mult), sDVE, 1)
                                psU_free[pb] = (sDVE, sDVE.n)
                                em.wait("vector", sDVE, sDVE.n)
                                em.op("vector", lambda e, ub=ub, tw=tw % 4, uc=uc, e_=e_: e.tensor_scalar(
                                    ust[ub][:, tw * 512:(tw + 1) * 512],
                                    ust[ub][:, tw * 512:(tw + 1) * 512],
                                    bup_sb[:, uc * NE + e_: uc * NE + e_ + 1], None,
                                    ALU.add), sDVE, 1)
                                if tw % 4 == 3:
                                    em.wait("sync", sDVE, sDVE.n)
                                    em.op("sync", lambda e, ub=ub, e_=e_, uc=uc, th=tw // 4: e.dma_start(
                                        dap(u_hbm, (e_ * D + uc * 128) * S + th * 2048,
                                            [[S, 128], [1, 2048]]),
                                        ust[ub][:]), sSt, 16)
                                    ust_free[ub] = sSt.n
                        wslab_mark[nslab] = sPE2.n
                        nslab += 1
                em.flush("p3")

        if "p4" in PHASES:
            p4 = ExitStack()
            with p4:
                wg = [sbt(f"p4w{i}", [128, KC * D], BF16, p4) for i in range(2)]
                usl = [sbt(f"p4u{i}", [128, KC * 256], F32, p4) for i in range(2)]
                ubf = [sbt(f"p4ub{i}", [128, KC * 256], BF16, p4) for i in range(2)]
                sgt = [sbt(f"p4sg{i}", [128, 512], F32, p4) for i in range(2)]
                gst = [sbt(f"p4g{i}", [128, 8 * 256], BF16, p4) for i in range(2)]
                prow = sbt("p4p", [1, S], BF16, p4)
                pbc = sbt("p4pb", [128, 256], F32, p4)
                psG = [psumt(f"p4p{i}", [128, 512], F32, p4) for i in range(2)]
                psP = psumt("p4pp", [128, 512], F32, p4)

                wg_free = [0, 0]
                usl_free = [0, 0]
                ubf_free = [0, 0]
                gst_free = [0, 0]
                sgt_free = [(None, 0), (None, 0)]
                psG_free = {0: (None, 0), 1: (None, 0)}
                psP_free = (None, 0)
                pbc_free = 0

                ub_v_tab = {}

                def load_usl(e_, tw):
                    twb = tw % 2
                    em.wait("sync", sDVE, usl_free[twb])
                    em.op("sync", lambda e, e_=e_, tw=tw, twb=twb: e.dma_start(
                        usl[twb][:].rearrange("p (c n) -> p c n", c=KC),
                        dap(u_hbm, e_ * D * S + tw * 256,
                            [[S, 128], [128 * S, KC], [1, 256]])), sL, 16)
                    return sL.n

                def load_ubf(e_, tw):
                    tb3 = (e_ * 16 + tw) % 2
                    em.wait("gpsimd", sPE2, ubf_free[tb3])
                    em.op("gpsimd", lambda e, e_=e_, tw=tw, tb3=tb3: e.dma_start(
                        ubf[tb3][:].rearrange("p (c n) -> p c n", c=KC),
                        dap(u_hbm, e_ * D * S + tw * 256,
                            [[S, 128], [128 * S, KC], [1, 256]])), sL, 16)
                    ub_v_tab[(e_, tw)] = sL.n

                prow_v_tab = {}
                p4_sPE_base = sPE.n

                def load_prow(e_):
                    if e_ == 0:
                        em.wait("scalar", sW, xmark["prow_w"])
                    else:
                        em.wait("scalar", sPE, p4_sPE_base + 16 * e_)
                    em.op("scalar", lambda e, e_=e_: e.dma_start(
                        prow[:], prow_hbm.ap()[e_:e_ + 1, :]), sX, 16)
                    prow_v_tab[e_] = sX.n

                wg_v_tab = {}

                def load_wg_slice(e_, q):
                    eb = e_ % 2
                    if q == 0:
                        em.wait("gpsimd", sPE2, wg_free[eb])
                    em.op("gpsimd", lambda e, e_=e_, eb=eb, q=q: e.dma_start(
                        wg[eb][:].rearrange("p (c n) -> p c n", c=KC)
                        [:, :, q * 256:(q + 1) * 256],
                        dap(wgate_in, e_ * D * D + q * 256,
                            [[D, 128], [128 * D, KC], [1, 256]])), sW, 16)
                    wg_v_tab[e_] = sW.n

                u_v_next = load_usl(0, 0)
                load_ubf(0, 0)
                load_prow(0)
                for q in range(8):
                    load_wg_slice(0, q)
                for e_ in range(NE):
                    eb = e_ % 2
                    wg_v = wg_v_tab[e_]
                    prow_v = prow_v_tab[e_]
                    for tw in range(16):
                        twb = tw % 2
                        tb3 = (e_ * 16 + tw) % 2
                        u_v = u_v_next
                        if tw < 15:
                            u_v_next = load_usl(e_, tw + 1)
                        elif e_ < NE - 1:
                            u_v_next = load_usl(e_ + 1, 0)
                        if tw + 1 < 16:
                            load_ubf(e_, tw + 1)
                        elif e_ < NE - 1:
                            load_ubf(e_ + 1, 0)
                        ub_v = ub_v_tab[(e_, tw)]
                        _wgq = {2: 0, 3: 1, 5: 2, 6: 3, 8: 4, 9: 5, 11: 6, 12: 7}
                        if tw in _wgq and e_ + 1 < NE:
                            load_wg_slice(e_ + 1, _wgq[tw])
                        em.wait("tensor", sX, prow_v)
                        if psP_free[0] is not None:
                            em.wait("tensor", psP_free[0], psP_free[1])
                        em.op("tensor", lambda e, tw=tw: e.matmul(
                            psP[:, 0:256], onesb[0:1, :], prow[0:1, tw * 256:(tw + 1) * 256],
                            start=True, stop=True), sPE, 1)
                        em.wait("scalar", sPE, sPE.n)
                        em.wait("scalar", sDVE, pbc_free)
                        em.op("scalar", lambda e: e.copy(pbc[:], psP[:, 0:256]), sACT2, 1)
                        psP_free = (sACT2, sACT2.n)
                        pbc_v = sACT2.n
                        if tw == 0:
                            em.wait("tensor", sW, wg_v)
                        em.wait("tensor", sL, ub_v)
                        for gp in range(8):
                            gc0 = 2 * gp
                            pb = gp % 2
                            ws, wv = psG_free[pb]
                            if ws is not None:
                                em.wait("tensor", ws, wv)
                            for gph in range(2):
                                gc = gc0 + gph
                                for kc in range(KC):
                                    em.op("tensor", lambda e, pb=pb, gph=gph, gc=gc, kc=kc, eb=eb, twb=tb3: e.matmul(
                                        psG[pb][:, gph * 256:(gph + 1) * 256],
                                        wg[eb][:, kc * D + gc * 128: kc * D + (gc + 1) * 128],
                                        ubf[twb][:, kc * 256:(kc + 1) * 256],
                                        start=(kc == 0), stop=(kc == KC - 1)),
                                        sPE2 if (kc == KC - 1 and gph == 1) else None, 1)
                            mm_v = sPE2.n
                            em.wait("scalar", sPE2, mm_v)
                            ws, wv = sgt_free[pb]
                            if ws is not None:
                                em.wait("scalar", ws, wv)
                            for gph in range(2):
                                gc = gc0 + gph
                                em.op("scalar", lambda e, pb=pb, gph=gph, gc=gc, e_=e_: e.activation(
                                    sgt[pb][:, gph * 256:(gph + 1) * 256],
                                    psG[pb][:, gph * 256:(gph + 1) * 256], AF.Silu,
                                    bias=bgate_sb[:, gc * NE + e_: gc * NE + e_ + 1]),
                                    sACT, 1)
                            psG_free[pb] = (sACT, sACT.n)
                            em.wait("vector", sACT, sACT.n)
                            if gp == 0:
                                em.wait("vector", sL, u_v)
                                em.wait("vector", sACT2, pbc_v)
                            hb4 = gp // 4
                            if gp % 4 == 0:
                                em.wait("vector", sSt, gst_free[hb4])
                            em.op("vector", lambda e, pb=pb, gc0=gc0, twb=twb: e.tensor_tensor(
                                sgt[pb][:], sgt[pb][:],
                                usl[twb][:, gc0 * 256: gc0 * 256 + 512], ALU.mult), sDVE, 1)
                            em.wait("vector", sDVE, sDVE.n)
                            pbc2 = dap(pbc, 0, [[256, 128], [0, 2], [1, 256]])
                            em.op("vector", lambda e, pb=pb, hb4=hb4, lc0=gc0 % 8, p2_=pbc2: e.tensor_tensor(
                                gst[hb4][:].rearrange("p (c n) -> p c n", c=8)
                                [:, lc0:lc0 + 2, :], sgt[pb][:].rearrange(
                                    "p (c n) -> p c n", c=2), p2_, ALU.mult), sDVE, 1)
                            sgt_free[pb] = (sDVE, sDVE.n)
                            if gp % 4 == 3:
                                em.wait("sync", sDVE, sDVE.n)
                                em.op("sync", lambda e, e_=e_, tw=tw, hb4=hb4: e.dma_start(
                                    dap(g_hbm, e_ * D * S + hb4 * 8 * 128 * S + tw * 256,
                                        [[S, 128], [128 * S, 8], [1, 256]]),
                                    gst[hb4][:].rearrange("p (c n) -> p c n", c=8)), sSt, 16)
                                gst_free[hb4] = sSt.n
                        usl_free[twb] = sDVE.n
                        ubf_free[tb3] = sPE2.n
                        pbc_free = sDVE.n
                    wg_free[eb] = sPE2.n
                    if e_ + 1 < NE:
                        load_prow(e_ + 1)
                em.flush("p4")


        if "p5" in PHASES:
            p5 = ExitStack()
            with p5:
                bdmat_sb = sbt("bdmat_sb", [8, D], F32, p5)
                probs8 = sbt("probs8_5", [8, S], F32, p5)
                oacc = sbt("p5o", [128, KC * 1024], F32, p5)
                gsl = [sbt(f"p5g{i}", [128, KC * 1024], BF16, p5) for i in range(2)]
                wdl = [sbt(f"p5w{i}", [128, KC * 128], BF16, p5) for i in range(4)]
                x2t = [sbt(f"p5x{i}", [128, D], F32, p5) for i in range(2)]
                ost = [sbt(f"p5s{i}", [128, D], F32, p5) for i in range(2)]
                psD = [psumt(f"p5p{i}", [128, 512], F32, p5) for i in range(4)]
                ptf5 = [psumt(f"p5t{i}", [128, 512], F32, p5) for i in range(2)]

                em.op("sync", lambda e: e.dma_start(bdmat_sb[:], bdmat_in.ap()), sL, 16)
                em.op("sync", lambda e: e.dma_start(probs8[:], probs8f_hbm.ap()), sL, 16)
                bd_v = sL.n
                gsl_free = [0, 0]
                wdl_mark = {}
                psD_free = {0: (None, 0), 1: (None, 0), 2: (None, 0), 3: (None, 0)}
                ptf_free = {0: (None, 0), 1: (None, 0)}
                x2_free = [0, 0]
                ost_free = [0, 0]
                nwd = 0
                for grp in range(4):
                    for e_ in range(NE):
                        gb = e_ % 2
                        em.wait("sync", sPE2, gsl_free[gb])
                        em.op("sync", lambda e, e_=e_, grp=grp, gb=gb: e.dma_start(
                            gsl[gb][:].rearrange("p (c n) -> p c n", c=KC),
                            dap(g_hbm, e_ * D * S + grp * 1024,
                                [[S, 128], [128 * S, KC], [1, 1024]])), sL, 16)
                        g_v = sL.n
                        for cc in range(KC):
                            wb = nwd % 4
                            wq = "gpsimd"
                            if nwd >= 4:
                                em.wait(wq, sPE2, wdl_mark[nwd - 4])
                            em.op(wq, lambda e, e_=e_, cc=cc, wb=wb: e.dma_start(
                                wdl[wb][:].rearrange("p (c n) -> p c n", c=KC),
                                dap(wdown_in, e_ * D * D + cc * 128,
                                    [[D, 128], [128 * D, KC], [1, 128]])), sW, 16)
                            w_v = sW.n
                            pb0 = (cc % 2) * 2
                            em.wait("tensor", sW, w_v)
                            if cc == 0:
                                em.wait("tensor", sL, g_v)
                                if grp == 0 and e_ == 0:
                                    em.wait("tensor", sL, bd_v)
                            for sw in range(2):
                                ws, wv = psD_free[pb0 + sw]
                                if ws is not None:
                                    em.wait("tensor", ws, wv)
                            for kc in range(KC):
                                for sw in range(2):
                                    last = (kc == KC - 1) and (e_ != 0)
                                    em.op("tensor", lambda e, pb=pb0 + sw, wb=wb, gb=gb, kc=kc, sw=sw, last=last: e.matmul(
                                        psD[pb][:],
                                        wdl[wb][:, kc * 128:(kc + 1) * 128],
                                        gsl[gb][:, kc * 1024 + sw * 512: kc * 1024 + (sw + 1) * 512],
                                        start=(kc == 0), stop=last),
                                        sPE2 if last else None, 1)
                            if e_ == 0:
                                for sw in range(2):
                                    em.op("tensor", lambda e, pb=pb0 + sw, cc=cc, grp=grp, sw=sw: e.matmul(
                                        psD[pb][:], bdmat_sb[:, cc * 128:(cc + 1) * 128],
                                        probs8[:, grp * 1024 + sw * 512: grp * 1024 + sw * 512 + 512],
                                        start=False, stop=True), sPE2, 1)
                            mm_v = sPE2.n
                            em.wait("vector", sPE2, mm_v)
                            for sw in range(2):
                                if e_ == 0:
                                    em.op("vector", lambda e, pb=pb0 + sw, cc=cc, sw=sw: e.tensor_copy(
                                        oacc[:, cc * 1024 + sw * 512: cc * 1024 + (sw + 1) * 512],
                                        psD[pb][:]), sDVE, 1)
                                else:
                                    em.op("vector", lambda e, pb=pb0 + sw, cc=cc, sw=sw: e.tensor_tensor(
                                        oacc[:, cc * 1024 + sw * 512: cc * 1024 + (sw + 1) * 512],
                                        oacc[:, cc * 1024 + sw * 512: cc * 1024 + (sw + 1) * 512],
                                        psD[pb][:], ALU.add), sDVE, 1)
                                psD_free[pb0 + sw] = (sDVE, sDVE.n)
                            wdl_mark[nwd] = sPE2.n
                            nwd += 1
                        gsl_free[gb] = sPE2.n
                    acc_done = sDVE.n
                    x2v_tab = {}

                    def load_x2(tj):
                        tb = tj % 2
                        em.wait("sync", sDVE2, x2_free[tb])
                        em.op("sync", lambda e, tt=grp * 8 + tj, tb=tb: e.dma_start(
                            x2t[tb][:], x2_hbm.ap()[tt * 128:(tt + 1) * 128, :]), sL, 16)
                        x2v_tab[tj] = sL.n

                    load_x2(0)
                    load_x2(1)
                    for tj in range(8):
                        tt = grp * 8 + tj
                        tb = tj % 2
                        x2_v = x2v_tab[tj]
                        em.wait("vector", sL, x2_v)
                        if tj == 0:
                            em.wait("tensor", sDVE, acc_done)
                        for cg in range(4):
                            pb = cg % 2
                            ws, wv = ptf_free[pb]
                            if ws is not None:
                                em.wait("tensor", ws, wv)
                            for i4 in range(4):
                                cc = cg * 4 + i4
                                em.op("tensor", lambda e, pb=pb, cc=cc, i4=i4, tj=tj: e.transpose(
                                    ptf5[pb][:, i4 * 128:(i4 + 1) * 128],
                                    oacc[:, cc * 1024 + tj * 128: cc * 1024 + (tj + 1) * 128],
                                    identf[:]), sPE if i4 == 3 else None, 1)
                            em.wait("vector", sPE, sPE.n)
                            if cg == 0:
                                em.wait("vector", sSt, ost_free[tb])
                            em.op("vector", lambda e, pb=pb, cg=cg, tb=tb: e.tensor_tensor(
                                ost[tb][:, cg * 512:(cg + 1) * 512], ptf5[pb][:],
                                x2t[tb][:, cg * 512:(cg + 1) * 512], ALU.add), sDVE2, 1)
                            ptf_free[pb] = (sDVE2, sDVE2.n)
                        em.wait("sync", sDVE2, sDVE2.n)
                        em.op("sync", lambda e, tt=tt, tb=tb: e.dma_start(
                            x_out.ap()[tt * 128:(tt + 1) * 128, :], ost[tb][:]), sSt, 16)
                        ost_free[tb] = sSt.n
                        x2_free[tb] = sDVE2.n
                        if tj + 2 < 8:
                            load_x2(tj + 2)
                em.wait("sync", sSt, sSt.n)
                em.wait("gpsimd", sW, sW.n)
                em.flush("p5")
        else:
            em.wait("sync", sSt, sSt.n)
            em.wait("gpsimd", sW, sW.n)
            em.flush("tail")

    return nc


def _prep_inputs(L0_attn_w_qkv, L0_attn_w_out, L0_ffn_Wr, L0_ffn_br,
                 L0_ffn_gate_bias, L0_ffn_Wup, L0_ffn_bup, L0_ffn_Wgate,
                 L0_ffn_bgate, L0_ffn_Wdown, L0_ffn_bdown, L0_ffn_Wsu,
                 L0_ffn_bsu, L0_ffn_Wsg, L0_ffn_bsg, L0_ffn_Wsd, L0_ffn_bsd):
    f32 = np.float32
    bf16 = ml_dtypes.bfloat16

    def src_feature(kind, h, d):
        m, j = divmod(h, 4)
        off = {"q": 0, "v": 512, "k": 1024}[kind]
        return m * 1536 + off + j * 128 + d

    def rot_src(dp):
        if dp < 32:
            return 2 * dp
        if dp < 64:
            return 2 * (dp - 32) + 1
        return dp

    perm = np.empty(3 * D, dtype=np.int64)
    for h in range(NH):
        for dp in range(HD):
            perm[h * HD + dp] = src_feature("q", h, rot_src(dp))
            perm[D + h * HD + dp] = src_feature("k", h, rot_src(dp))
            perm[2 * D + h * HD + dp] = src_feature("v", h, dp)
    wqkvT = np.ascontiguousarray(np.asarray(L0_attn_w_qkv)[perm, :].T, dtype=f32)
    woutT = np.ascontiguousarray(np.asarray(L0_attn_w_out).T, dtype=f32)

    wup = np.ascontiguousarray(np.concatenate(
        [np.asarray(L0_ffn_Wup), np.asarray(L0_ffn_Wsu)[None]], axis=0), dtype=f32)
    wgate = np.ascontiguousarray(np.concatenate(
        [np.asarray(L0_ffn_Wgate), np.asarray(L0_ffn_Wsg)[None]], axis=0), dtype=f32)
    wdown = np.ascontiguousarray(np.concatenate(
        [np.asarray(L0_ffn_Wdown), np.asarray(L0_ffn_Wsd)[None]], axis=0), dtype=f32)
    bup8 = np.concatenate([np.asarray(L0_ffn_bup),
                           np.asarray(L0_ffn_bsu)[None]], axis=0).astype(f32)
    bgate8 = np.concatenate([np.asarray(L0_ffn_bgate),
                             np.asarray(L0_ffn_bsg)[None]], axis=0).astype(f32)
    bdmat = np.concatenate([np.asarray(L0_ffn_bdown),
                            np.asarray(L0_ffn_bsd)[None]], axis=0).astype(f32)
    bup_t = np.ascontiguousarray(bup8.T.reshape(KC, 128, NE))
    bgate_t = np.ascontiguousarray(bgate8.T.reshape(KC, 128, NE))

    inv_freq = (10000.0 ** (-np.arange(0, 64, 2, dtype=np.float64) / 64.0))
    freqs = np.arange(S, dtype=np.float64)[:, None] * inv_freq[None, :]
    cosv = np.cos(freqs).astype(f32)
    sinv = np.sin(freqs).astype(f32)
    costab = np.ascontiguousarray(
        cosv.reshape(32, 128, ROT).transpose(1, 0, 2).reshape(128, 32 * ROT))
    sintab = np.ascontiguousarray(
        sinv.reshape(32, 128, ROT).transpose(1, 0, 2).reshape(128, 32 * ROT))

    masks = np.zeros((4, 128, 512), dtype=bf16)
    qq = np.arange(512)[None, :]
    kk = np.arange(128)[:, None]
    for j in range(4):
        masks[j] = (qq >= kk + j * 128).astype(bf16)
    masks2 = np.concatenate(
        [np.concatenate([masks[0], masks[1]], axis=1),
         np.concatenate([masks[2], masks[3]], axis=1)], axis=0)

    return {
        "wqkvT": wqkvT, "woutT": woutT,
        "wr_in": np.ascontiguousarray(np.asarray(L0_ffn_Wr), dtype=f32),
        "wup_in": wup, "wgate_in": wgate, "wdown_in": wdown,
        "bup_in": bup_t, "bgate_in": bgate_t, "bdmat_in": bdmat,
        "costab_in": costab, "sintab_in": sintab,
        "br_in": np.ascontiguousarray(
            np.broadcast_to(np.asarray(L0_ffn_br, dtype=f32), (128, 7))),
        "gb_in": np.ascontiguousarray(
            np.broadcast_to(np.asarray(L0_ffn_gate_bias, dtype=f32), (128, 7))),
        "identb_in": np.eye(128, dtype=bf16),
        "identf_in": np.eye(128, dtype=f32),
        "onesb_in": np.ones((128, 128), dtype=bf16),
        "onesf_in": np.ones((1, 128), dtype=f32),
        "masks_in": np.ascontiguousarray(masks2.reshape(2 * 128, 1024)),
    }


def kernel(L0_attn_hidden_states, **weights):
    x = np.asarray(L0_attn_hidden_states, dtype=np.float32)
    B = x.shape[0]
    common = _prep_inputs(**weights)
    nc = build_program(debug=DEBUG)
    in_maps = []
    for b in range(B):
        m = dict(common)
        m["x_in"] = np.ascontiguousarray(x[b])
        in_maps.append(m)
    res = run_bass_kernel_spmd(nc, in_maps, core_ids=list(range(B)))
    return np.stack([np.asarray(r["x_out"]) for r in res.results], axis=0)


# version 12
# speedup vs baseline: 1.1391x; 1.1391x over previous
import math
from contextlib import ExitStack

import numpy as np
import ml_dtypes
import concourse.bass as bass
import concourse.mybir as mybir
from concourse.bass_utils import run_bass_kernel_spmd

F32 = mybir.dt.float32
BF16 = mybir.dt.bfloat16
AF = mybir.ActivationFunctionType
ALU = mybir.AluOpType
AX = mybir.AxisListType

S = 4096
D = 2048
NH = 16
HD = 128
KC = 16
NE = 8
ROT = 32
EPS = 1e-5
ISQ = 1.0 / math.sqrt(128.0)
BIGL = 1024.0

PHASES = ("p0", "p1", "p2", "p3", "p4", "p5")
DEBUG = ()


class Sem:
    def __init__(self, h):
        self.h = h
        self.n = 0


class Emitter:
    ENGS = ("sync", "gpsimd", "tensor", "scalar", "vector")

    def __init__(self, nc):
        self.nc = nc
        self.q = {k: [] for k in self.ENGS}

    def wait(self, eng, sem, val):
        assert val <= sem.n, f"wait on {val} but only {sem.n} scheduled ({eng})"
        if val <= 0:
            return
        self.q[eng].append(lambda e, h=sem.h, v=val: e.wait_ge(h, v))

    def op(self, eng, fn, sem=None, k=1):
        if sem is not None:
            sem.n += k
            self.q[eng].append(lambda e, f=fn, h=sem.h, kk=k: f(e).then_inc(h, kk))
        else:
            self.q[eng].append(lambda e, f=fn: f(e))

    def flush(self, name):
        with self.nc.Block(name) as blk:
            for eng in self.ENGS:
                fns = self.q[eng]
                if fns:
                    getattr(blk, eng)(lambda e, t=tuple(fns): [f(e) for f in t])
        self.q = {k: [] for k in self.ENGS}


def dap(t, offset, dims):
    return bass.AP(t, offset, [list(d) for d in dims])


def build_program(debug=()):
    nc = bass.Bass("TRN2", target_bir_lowering=False)

    def scratch(name, shape, dtype):
        kind = "ExternalOutput" if name in debug else "Internal"
        return nc.dram_tensor(name, shape, dtype, kind=kind)

    x_in = nc.dram_tensor("x_in", [S, D], F32, kind="ExternalInput")
    wqkvT = nc.dram_tensor("wqkvT", [D, 3 * D], F32, kind="ExternalInput")
    woutT = nc.dram_tensor("woutT", [D, D], F32, kind="ExternalInput")
    wr_in = nc.dram_tensor("wr_in", [D, 7], F32, kind="ExternalInput")
    wup_in = nc.dram_tensor("wup_in", [NE, D, D], F32, kind="ExternalInput")
    wgate_in = nc.dram_tensor("wgate_in", [NE, D, D], F32, kind="ExternalInput")
    wdown_in = nc.dram_tensor("wdown_in", [NE, D, D], F32, kind="ExternalInput")
    bup_in = nc.dram_tensor("bup_in", [KC, 128, NE], F32, kind="ExternalInput")
    bgate_in = nc.dram_tensor("bgate_in", [KC, 128, NE], F32, kind="ExternalInput")
    bdmat_in = nc.dram_tensor("bdmat_in", [NE, D], F32, kind="ExternalInput")
    costab_in = nc.dram_tensor("costab_in", [128, 32 * ROT], F32, kind="ExternalInput")
    sintab_in = nc.dram_tensor("sintab_in", [128, 32 * ROT], F32, kind="ExternalInput")
    br_in = nc.dram_tensor("br_in", [128, 7], F32, kind="ExternalInput")
    gb_in = nc.dram_tensor("gb_in", [128, 7], F32, kind="ExternalInput")
    identb_in = nc.dram_tensor("identb_in", [128, 128], BF16, kind="ExternalInput")
    identf_in = nc.dram_tensor("identf_in", [128, 128], F32, kind="ExternalInput")
    onesb_in = nc.dram_tensor("onesb_in", [128, 128], BF16, kind="ExternalInput")
    onesf_in = nc.dram_tensor("onesf_in", [1, 128], F32, kind="ExternalInput")
    masks_in = nc.dram_tensor("masks_in", [2 * 128, 1024], BF16, kind="ExternalInput")

    q_hbm = scratch("q_hbm", [D, S], BF16)
    k_hbm = scratch("k_hbm", [D, S], BF16)
    v_hbm = scratch("v_hbm", [S, D], BF16)
    attn_hbm = scratch("attn_hbm", [D, S], BF16)
    x2_hbm = scratch("x2_hbm", [S, D], F32)
    xg_hbm = scratch("xg_hbm", [D, S], BF16)
    u_hbm = scratch("u_hbm", [NE * D, S], F32)
    g_hbm = scratch("g_hbm", [NE * D, S], BF16)
    prow_hbm = scratch("prow_hbm", [NE, S], BF16)
    r_hbm = scratch("r_hbm", [32, 128], F32)
    probs8f_hbm = scratch("probs8f_hbm", [NE, S], F32)
    x_out = nc.dram_tensor("x_out", [S, D], F32, kind="ExternalOutput")

    st = ExitStack()

    def sbt(name, shape, dtype, stack=None):
        return (stack or st).enter_context(nc.sbuf_tensor(name, shape, dtype))

    def psumt(name, shape, dtype, stack=None):
        return (stack or st).enter_context(nc.psum_tensor(name, shape, dtype))

    with st:
        constf_t = sbt("constf", [128, 560], F32)
        constb_t = sbt("constb", [128, 384], BF16)
        bup_sb = constf_t[:, 0:128]
        bgate_sb = constf_t[:, 128:256]
        br_bc = constf_t[:, 256:263]
        gb_bc = constf_t[:, 263:270]
        epsb = constf_t[:, 270:271]
        mxx = constf_t[:, 272:304]
        identf = constf_t[:, 304:432]
        onesf = constf_t[0:1, 432:560]
        identb = constb_t[:, 0:128]
        onesb = constb_t[:, 128:256]
        wr_sb = constb_t[:, 256:368]

        sems = {n: Sem(st.enter_context(nc.semaphore(n))) for n in
                ("sL", "sSt", "sW", "sPE", "sPE2", "sACT", "sACT2", "sDVE", "sDVE2", "sX")}
        sL, sSt, sW = sems["sL"], sems["sSt"], sems["sW"]
        sPE, sPE2 = sems["sPE"], sems["sPE2"]
        sACT, sACT2 = sems["sACT"], sems["sACT2"]
        sDVE, sDVE2 = sems["sDVE"], sems["sDVE2"]
        sX = sems["sX"]

        em = Emitter(nc)
        xmark = {}

        for dst, src in ((identb, identb_in), (identf, identf_in),
                         (onesb, onesb_in), (onesf, onesf_in),
                         (br_bc, br_in), (gb_bc, gb_in)):
            em.op("sync", lambda e, d=dst, s=src: e.dma_start(d[:], s.ap()), sL, 16)
        em.op("sync", lambda e: e.dma_start(
            bup_sb[:].rearrange("p (c n) -> p c n", c=KC),
            bup_in.ap().rearrange("c p n -> p c n")), sL, 16)
        em.op("sync", lambda e: e.dma_start(
            bgate_sb[:].rearrange("p (c n) -> p c n", c=KC),
            bgate_in.ap().rearrange("c p n -> p c n")), sL, 16)
        em.op("gpsimd", lambda e: e.dma_start(
            wr_sb[:].rearrange("p (c n) -> p c n", c=KC),
            wr_in.ap().rearrange("(c p) n -> p c n", p=128)), sW, 16)
        em.op("vector", lambda e: e.memset(epsb[:], EPS), sDVE, 1)
        em.op("vector", lambda e: e.memset(mxx[:], 0.0), sDVE, 1)
        em.flush("setup")

        if "p0" in PHASES:
            p0 = ExitStack()
            with p0:
                costab = sbt("costab", [128, 32 * ROT], F32, p0)
                sintab = sbt("sintab", [128, 32 * ROT], F32, p0)
                xln_fm = sbt("xln_fm", [128, KC * (S // 2)], BF16, p0)
                x_t = [sbt(f"p0x{i}", [128, D], F32, p0) for i in range(2)]
                stt = [sbt(f"p0s{i}", [128, 16], F32, p0) for i in range(2)]
                xln_tm = [sbt(f"p0n{i}", [128, D], BF16, p0) for i in range(2)]
                sq_scr = sbt("p0sq", [128, D], F32, p0)
                rope_t = sbt("p0rt", [128, 512], F32, p0)
                wbf = [sbt(f"p0w{i}", [128, KC * 512], BF16, p0) for i in range(2)]
                scr_f = [sbt(f"p0f{i}", [128, 512], F32, p0) for i in range(3)]
                tm_bf = [sbt(f"p0b{i}", [128, 512], BF16, p0) for i in range(3)]
                qk_stage = sbt("p0qs", [128, 4 * (S // 2)], BF16, p0)
                sqdump = sbt("p0sd", [128, 512], F32, p0)
                sqtmp = [sbt(f"p0st{i}", [128, 4], F32, p0) for i in range(2)]
                rr_all = sbt("p0rr", [128, 32], F32, p0)
                v_bf = [sbt(f"p0v{i}", [128, 512], BF16, p0) for i in range(2)]
                psA = [psumt(f"p0pA{i}", [128, 512], F32, p0) for i in range(2)]
                ptb = [psumt(f"p0pT{i}", [128, 1024], BF16, p0) for i in range(2)]

                em.op("sync", lambda e: e.dma_start(costab[:], costab_in.ap()), sL, 16)
                em.op("sync", lambda e: e.dma_start(sintab[:], sintab_in.ap()), sL, 16)
                tab_v = sL.n

                def c(b, i):
                    return stt[b][:, i:i + 1]

                act_mark = {}
                x_sq_mark = {}
                r_mark = {}
                pe_mark = {}
                pt_free = {0: (None, 0), 1: (None, 0)}
                psA_free = {0: (None, 0), 1: (None, 0)}
                stage_free = [0, 0, 0, 0]
                vst_free = [0, 0]
                ropetm_free = [(None, 0), (None, 0), (None, 0)]
                sqtmp_free = [(None, 0), (None, 0)]
                mm_after_half = []
                mm_slab_mark = {}
                stage_free_wc = 0
                sq_mark = {}

                for half in range(2):
                    base_t = half * 16
                    for ti in range(16):
                        tg = base_t + ti
                        b = ti % 2
                        if tg >= 2:
                            em.wait("sync", sDVE2, act_mark[tg - 2])
                            em.wait("sync", sACT, x_sq_mark[tg - 2])
                        em.op("sync", lambda e, b=b, tg=tg: e.dma_start(
                            x_t[b][:], x_in.ap()[tg * 128:(tg + 1) * 128, :]), sL, 16)
                        ldv = sL.n
                        em.wait("vector", sL, ldv)
                        em.op("vector", lambda e, b=b: e.reduce_sum(
                            c(b, 0), x_t[b][:], AX.X), sDVE2, 1)
                        em.wait("vector", sDVE2, sDVE2.n)
                        em.op("vector", lambda e, b=b: e.tensor_scalar(
                            c(b, 2), c(b, 0), 1.0 / D, None, ALU.mult), sDVE2, 1)
                        em.wait("vector", sDVE2, sDVE2.n)
                        if tg >= 2:
                            em.wait("vector", sPE, pe_mark[tg - 2])
                        em.op("vector", lambda e, b=b: e.tensor_scalar(
                            xln_tm[b][:], x_t[b][:], c(b, 2), None,
                            ALU.subtract), sDVE2, 1)
                        act_mark[tg] = sDVE2.n
                        em.wait("scalar", sL, ldv)
                        if tg >= 1:
                            em.wait("scalar", sDVE2, sq_mark[tg - 1])
                        em.op("scalar", lambda e, b=b: e.activation(
                            sq_scr[:], x_t[b][:], AF.Square), sACT, 1)
                        x_sq_mark[tg] = sACT.n
                        em.wait("vector", sACT, sACT.n)
                        em.op("vector", lambda e, b=b: e.reduce_sum(
                            c(b, 1), sq_scr[:], AX.X), sDVE2, 1)
                        sq_mark[tg] = sDVE2.n
                        em.wait("vector", sDVE2, sDVE2.n)
                        em.op("vector", lambda e, b=b: e.tensor_scalar(
                            c(b, 4), c(b, 1), 1.0 / D, None, ALU.mult), sDVE2, 1)
                        em.op("vector", lambda e, b=b: e.tensor_tensor(
                            c(b, 3), c(b, 2), c(b, 2), ALU.mult), sDVE2, 1)
                        em.wait("vector", sDVE2, sDVE2.n)
                        em.op("vector", lambda e, b=b: e.tensor_sub(
                            c(b, 5), c(b, 4), c(b, 3)), sDVE2, 1)
                        var_v = sDVE2.n
                        em.wait("scalar", sDVE2, var_v)
                        em.op("scalar", lambda e, b=b: e.activation(
                            c(b, 6), c(b, 5), AF.Sqrt, bias=epsb[:]), sACT, 1)
                        em.wait("vector", sACT, sACT.n)
                        em.op("vector", lambda e, b=b, tg=tg: e.reciprocal(
                            rr_all[:, tg:tg + 1], c(b, 6)), sDVE2, 1)
                        r_mark[tg] = sDVE2.n
                        em.wait("tensor", sDVE2, act_mark[tg])
                        if half == 1 and ti == 0:
                            em.wait("vector", sPE2, mm_after_half[0])
                        for kb in range(4):
                            pb = kb % 2
                            ws, wv = pt_free[pb]
                            if ws is not None:
                                em.wait("tensor", ws, wv)
                            for j in range(4):
                                kc = kb * 4 + j
                                em.op("tensor", lambda e, b=b, kc=kc, j=j, pb=pb: e.transpose(
                                    ptb[pb][:, j * 128:(j + 1) * 128],
                                    xln_tm[b][:, kc * 128:(kc + 1) * 128],
                                    identb[:]), sPE if j == 3 else None, 1)
                            em.wait("vector", sPE, sPE.n)
                            em.op("vector", lambda e, kb=kb, ti=ti, pb=pb: e.tensor_copy(
                                dap(xln_fm, kb * 4 * 2048 + ti * 128,
                                    [[KC * 2048, 128], [2048, 4], [1, 128]]),
                                ptb[pb][:, 0:512]), sDVE, 1)
                            pt_free[pb] = (sDVE, sDVE.n)
                        pe_mark[tg] = sPE.n
                    xln_ready = sDVE.n

                    pend_tr = []

                    def emit_tr(ib, ti, rope_v, wc, tg):
                        em.wait("tensor", sDVE2, rope_v)
                        pb = ti % 2
                        ws, wv = pt_free[pb]
                        if ws is not None:
                            em.wait("tensor", ws, wv)
                        for hh in range(4):
                            em.op("tensor", lambda e, pb=pb, ib=ib, hh=hh: e.transpose(
                                ptb[pb][:, hh * 128:(hh + 1) * 128],
                                tm_bf[ib][:, hh * 128:(hh + 1) * 128],
                                identb[:]), sPE if hh == 3 else None, 1)
                        em.wait("vector", sPE, sPE.n)
                        if ti == 0:
                            em.wait("vector", sSt, stage_free_wc)
                        em.op("vector", lambda e, ti=ti, pb=pb: e.tensor_copy(
                            dap(qk_stage, ti * 128, [[4 * 2048, 128], [2048, 4], [1, 128]]),
                            ptb[pb][:, 0:512]), sDVE, 1)
                        pt_free[pb] = (sDVE, sDVE.n)
                        ropetm_free[ib] = (sDVE, sDVE.n)
                        em.wait("scalar", sDVE2, rope_v)
                        sb2 = ti % 2
                        ws, wv = sqtmp_free[sb2]
                        if ws is not None:
                            em.wait("scalar", ws, wv)
                        for hh in range(4):
                            em.op("scalar", lambda e, ib=ib, hh=hh, sb2=sb2: e.activation(
                                sqdump[:, hh * 128:(hh + 1) * 128],
                                tm_bf[ib][:, hh * 128:(hh + 1) * 128], AF.Square,
                                accum_out=sqtmp[sb2][:, hh:hh + 1]), sACT, 1)
                        base = (0 if wc < 4 else 16) + (wc % 4) * 4
                        em.wait("vector", sACT, sACT.n)
                        em.op("vector", lambda e, base=base, sb2=sb2: e.tensor_tensor(
                            mxx[:, base:base + 4], mxx[:, base:base + 4],
                            sqtmp[sb2][:], ALU.max), sDVE2, 1)
                        sqtmp_free[sb2] = (sDVE2, sDVE2.n)

                    for wc in range(12):
                        wbuf = (half * 12 + wc) % 2
                        nslab = half * 12 + wc
                        if nslab >= 2:
                            em.wait("gpsimd", sPE2, mm_slab_mark[nslab - 2])
                        em.op("gpsimd", lambda e, wbuf=wbuf, wc=wc: e.dma_start(
                            wbf[wbuf][:].rearrange("p (c n) -> p c n", c=KC),
                            wqkvT.ap().rearrange("(c p) n -> p c n", p=128)
                            [:, :, wc * 512:(wc + 1) * 512]), sW, 16)
                        w_v = sW.n
                        is_v = wc >= 8
                        for ti in range(16):
                            tg = base_t + ti
                            idx = nslab * 16 + ti
                            ib = idx % 2
                            ib3 = idx % 3
                            if ti == 0:
                                em.wait("tensor", sW, w_v)
                                em.wait("tensor", sDVE, xln_ready)
                            ws, wv = psA_free[ib]
                            if ws is not None:
                                em.wait("tensor", ws, wv)
                            for kc in range(KC):
                                em.op("tensor", lambda e, ib=ib, kc=kc, ti=ti, wbuf=wbuf: e.matmul(
                                    psA[ib][:],
                                    xln_fm[:, kc * 2048 + ti * 128: kc * 2048 + (ti + 1) * 128],
                                    wbf[wbuf][:, kc * 512:(kc + 1) * 512],
                                    start=(kc == 0), stop=(kc == KC - 1)),
                                    sPE2 if kc == KC - 1 else None, 1)
                            mm_v = sPE2.n
                            if is_v:
                                vc = wc - 8
                                em.wait("scalar", sPE2, mm_v)
                                em.wait("scalar", sSt, vst_free[ib])
                                em.wait("scalar", sDVE2, r_mark[tg])
                                em.op("scalar", lambda e, ib=ib, tg=tg: e.activation(
                                    v_bf[ib][:], psA[ib][:], AF.Copy,
                                    scale=rr_all[:, tg:tg + 1]), sACT2, 1)
                                psA_free[ib] = (sACT2, sACT2.n)
                                em.wait("sync", sACT2, sACT2.n)
                                em.op("sync", lambda e, ib=ib, tg=tg, vc=vc: e.dma_start(
                                    v_hbm.ap()[tg * 128:(tg + 1) * 128, vc * 512:(vc + 1) * 512],
                                    v_bf[ib][:]), sSt, 16)
                                vst_free[ib] = sSt.n
                            else:
                                em.wait("scalar", sPE2, mm_v)
                                ws, wv = ropetm_free[ib3]
                                if ws is not None:
                                    em.wait("scalar", ws, wv)
                                em.wait("scalar", sDVE2, r_mark[tg])
                                em.op("scalar", lambda e, ib=ib, ib3=ib3, tg=tg: e.activation(
                                    scr_f[ib3][:], psA[ib][:], AF.Copy,
                                    scale=rr_all[:, tg:tg + 1]), sACT2, 1)
                                em.op("scalar", lambda e, ib=ib, ib3=ib3, tg=tg: e.activation(
                                    tm_bf[ib3][:], psA[ib][:], AF.Copy,
                                    scale=rr_all[:, tg:tg + 1]), sACT2, 1)
                                cp_v = sACT2.n
                                psA_free[ib] = (sACT2, cp_v)
                                em.wait("vector", sACT2, cp_v)
                                if wc == 0 and ti == 0:
                                    em.wait("vector", sL, tab_v)

                                def r3(t, lo):
                                    return t[:].rearrange("p (h cc) -> p h cc", h=4)[:, :, lo:lo + 32]

                                ctb = dap(costab, tg * ROT, [[32 * ROT, 128], [0, 4], [1, ROT]])
                                stb = dap(sintab, tg * ROT, [[32 * ROT, 128], [0, 4], [1, ROT]])
                                tv = [dap(rope_t, 128 * i, [[512, 128], [32, 4], [1, ROT]])
                                      for i in range(4)]
                                x1 = r3(scr_f[ib3], 0)
                                x2 = r3(scr_f[ib3], 32)
                                em.op("vector", lambda e, x1=x1, c_=ctb, o=tv[0]: e.tensor_tensor(o, x1, c_, ALU.mult))
                                em.op("vector", lambda e, x2=x2, s_=stb, o=tv[1]: e.tensor_tensor(o, x2, s_, ALU.mult), sDVE2, 1)
                                em.op("vector", lambda e, x2=x2, c_=ctb, o=tv[2]: e.tensor_tensor(o, x2, c_, ALU.mult))
                                em.op("vector", lambda e, x1=x1, s_=stb, o=tv[3]: e.tensor_tensor(o, x1, s_, ALU.mult), sDVE2, 1)
                                y1 = r3(tm_bf[ib3], 0)
                                y2 = r3(tm_bf[ib3], 32)
                                em.wait("vector", sDVE2, sDVE2.n)
                                em.op("vector", lambda e, o=y1, a=tv[0], b2=tv[1]: e.tensor_tensor(o, a, b2, ALU.subtract))
                                em.op("vector", lambda e, o=y2, a=tv[2], b2=tv[3]: e.tensor_tensor(o, a, b2, ALU.add), sDVE2, 1)
                                rope_v = sDVE2.n
                                pend_tr.append((ib3, ti, rope_v, wc, tg))
                                if len(pend_tr) > 2:
                                    emit_tr(*pend_tr.pop(0))
                        while pend_tr:
                            emit_tr(*pend_tr.pop(0))
                        mm_slab_mark[nslab] = sPE2.n
                        if not is_v:
                            dstt = q_hbm if wc < 4 else k_hbm
                            for hh in range(4):
                                h = (wc % 4) * 4 + hh
                                em.wait("sync", sDVE, sDVE.n)
                                em.op("sync", lambda e, hh=hh, h=h, dstt=dstt, half=half: e.dma_start(
                                    dstt.ap()[h * 128:(h + 1) * 128,
                                              half * 2048:(half + 1) * 2048],
                                    qk_stage[:, hh * 2048:(hh + 1) * 2048]), sSt, 16)
                            stage_free_wc = sSt.n
                    mm_after_half.append(sPE2.n)
                xmark["sq_done"] = sDVE2.n
                em.flush("p0")

        if "p1" in PHASES:
            p1 = ExitStack()
            with p1:
                masks = sbt("masks", [128, 2 * 1024], BF16, p1)
                qf = [sbt(f"p1q{i}", [128, S], BF16, p1) for i in range(2)]
                kf = [sbt(f"p1k{i}", [128, S], BF16, p1) for i in range(2)]
                vsb = [sbt(f"p1v{i}", [128, S], BF16, p1) for i in range(2)]
                mx2 = sbt("p1m2", [128, 4], F32, p1)
                crow = sbt("p1cr", [1, 16], F32, p1)
                cneg_all = sbt("p1cn", [128, 16], F32, p1)
                pT = [sbt(f"p1p{i}", [128, 1024], BF16, p1) for i in range(3)]
                l_row = sbt("p1l", [1, 512], F32, p1)
                lbc = sbt("p1lb", [128, 512], F32, p1)
                lbs = sbt("p1lbs", [128, 512], F32, p1)
                ast = [sbt(f"p1a{i}", [128, S], BF16, p1) for i in range(2)]
                tsb = [sbt(f"p1ts{i}", [128, 512], BF16, p1) for i in range(5)]
                psS2 = [psumt(f"p1pS{i}", [128, 1024], F32, p1) for i in range(2)]
                psO = [psumt(f"p1pO{i}", [128, 512], F32, p1) for i in range(2)]
                psL = psumt("p1pL", [128, 512], F32, p1)
                psB = psumt("p1pB", [128, 512], F32, p1)

                em.op("sync", lambda e: e.dma_start(
                    masks[:].rearrange("p (j n) -> p j n", j=2),
                    masks_in.ap().rearrange("(j p) n -> p j n", p=128)), sL, 16)
                masks_v = sL.n

                def head_load(h, guard):
                    hb = h % 2
                    if guard is not None:
                        em.wait("sync", sPE2, guard)
                    em.op("sync", lambda e, hb=hb, h=h: e.dma_start(
                        qf[hb][:], q_hbm.ap()[h * 128:(h + 1) * 128, :]), sL, 16)
                    em.op("sync", lambda e, hb=hb, h=h: e.dma_start(
                        kf[hb][:], k_hbm.ap()[h * 128:(h + 1) * 128, :]), sL, 16)
                    em.op("sync", lambda e, hb=hb, h=h: e.dma_start(
                        vsb[hb][:], dap(v_hbm, h * 128, [[D, 128], [128 * D, 32], [1, 128]])),
                        sL, 16)
                    return sL.n

                h_load = [head_load(0, None), head_load(1, None)]
                ast_free = [0, 0]
                lbc_free = (None, 0)
                pT_free = [(None, 0), (None, 0), (None, 0)]
                tsb_free = [(None, 0)] * 5
                psS_free = [(None, 0), (None, 0)]
                psO_free = [(None, 0), (None, 0)]
                psL_free = [(None, 0)]
                first_mask = [True]

                em.op("vector", lambda e: e.memset(mx2[:], 0.0), sDVE, 1)
                em.wait("tensor", sDVE, sDVE.n)
                em.wait("tensor", sDVE2, xmark["sq_done"])
                em.op("tensor", lambda e: e.transpose(
                    psB[0:16, 0:128], mxx[:, 0:16], identf[:]))
                em.op("tensor", lambda e: e.transpose(
                    psB[0:16, 128:256], mxx[:, 16:32], identf[:]), sPE, 1)
                em.wait("vector", sPE, sPE.n)
                em.op("vector", lambda e: e.tensor_reduce(
                    mx2[0:16, 0:1], psB[0:16, 0:128], AX.X, ALU.max))
                em.op("vector", lambda e: e.tensor_reduce(
                    mx2[0:16, 1:2], psB[0:16, 128:256], AX.X, ALU.max), sDVE, 1)
                em.wait("vector", sDVE, sDVE.n)
                em.op("vector", lambda e: e.tensor_tensor(
                    mx2[0:16, 2:3], mx2[0:16, 0:1], mx2[0:16, 1:2], ALU.add), sDVE, 1)
                em.wait("vector", sDVE, sDVE.n)
                em.op("vector", lambda e: e.tensor_scalar(
                    mx2[0:16, 3:4], mx2[0:16, 2:3], -0.5 * ISQ, None, ALU.mult), sDVE, 1)
                em.wait("tensor", sDVE, sDVE.n)
                em.op("tensor", lambda e: e.transpose(
                    psB[0:1, 256:384], mx2[:, 3:4], identf[:]), sPE, 1)
                em.wait("vector", sPE, sPE.n)
                em.op("vector", lambda e: e.tensor_copy(crow[:], psB[0:1, 256:272]), sDVE, 1)
                em.wait("tensor", sDVE, sDVE.n)
                em.op("tensor", lambda e: e.matmul(
                    psB[:, 384:400], onesf[:], crow[:], start=True, stop=True), sPE, 1)
                em.wait("scalar", sPE, sPE.n)
                em.op("scalar", lambda e: e.copy(cneg_all[:], psB[:, 384:400]), sACT2, 1)
                cneg_v = sACT2.n
                psB_free = (sACT2, cneg_v)

                for h in range(NH):
                    hb = h % 2
                    for qg in range(8):
                        npair = 2 * qg + 2
                        ob = qg % 2
                        ws, wv = psO_free[ob]
                        if ws is not None:
                            em.wait("tensor", ws, wv)

                        def emit_pv(pr, pb, ki0, gate, npair=npair, ob=ob, hb=hb):
                            em.wait("tensor", gate[0], gate[1])
                            for half in range(2):
                                last = (pr == npair - 1) and (half == 1)
                                em.op("tensor", lambda e, ob=ob, hb=hb, pb=pb, ki0=ki0, half=half, pr=pr, last=last: e.matmul(
                                    psO[ob][:],
                                    vsb[hb][:, (ki0 + half) * 128:(ki0 + half + 1) * 128],
                                    pT[pb][:, half * 512:(half + 1) * 512],
                                    start=(pr == 0 and half == 0), stop=last),
                                    sPE2 if half == 1 else None, 1)
                            pT_free[pb] = (sPE2, sPE2.n)

                        def emit_l(pb3, ts_v, first, lastp):
                            if first:
                                ws, wv = psL_free[0]
                                if ws is not None:
                                    em.wait("tensor", ws, wv)
                            em.wait("tensor", sDVE2, ts_v)
                            em.op("tensor", lambda e, pb3=pb3, first=first, lastp=lastp: e.matmul(
                                psL[0:1, :], onesb[:, 0:1], tsb[pb3][:],
                                start=first, stop=lastp), sPE2, 1)
                            tsb_free[pb3] = (sPE2, sPE2.n)

                        pend = []
                        pendL = []
                        for pr in range(npair):
                            ki0 = 2 * pr
                            pb = pr % 3
                            pb2 = pr % 2
                            pb3 = pr % 5
                            if qg == 0 and pr == 0:
                                em.wait("tensor", sL, h_load[h])
                                if h == 0:
                                    em.wait("scalar", sACT2, cneg_v)
                            ws, wv = psS_free[pb2]
                            if ws is not None:
                                em.wait("tensor", ws, wv)
                            for half in range(2):
                                em.op("tensor", lambda e, pb=pb2, hb=hb, ki0=ki0, qg=qg, half=half: e.matmul(
                                    psS2[pb][:, half * 512:(half + 1) * 512],
                                    kf[hb][:, (ki0 + half) * 128:(ki0 + half + 1) * 128],
                                    qf[hb][:, qg * 512:(qg + 1) * 512],
                                    start=True, stop=True),
                                    sPE if half == 1 else None, 1)
                            smm_v = sPE.n
                            em.wait("scalar", sPE, smm_v)
                            ws, wv = pT_free[pb]
                            if ws is not None:
                                em.wait("scalar", ws, wv)
                            em.op("scalar", lambda e, pb=pb, pb2=pb2, h=h: e.activation(
                                pT[pb][:], psS2[pb2][:], AF.Exp, bias=cneg_all[:, h:h + 1],
                                scale=ISQ), sACT, 1)
                            exp_v = sACT.n
                            psS_free[pb2] = (sACT, exp_v)
                            jp = pr - (npair - 2)
                            if jp >= 0:
                                if first_mask[0]:
                                    em.wait("vector", sL, masks_v)
                                    first_mask[0] = False
                                em.wait("vector", sACT, exp_v)
                                em.op("vector", lambda e, pb=pb, jp=jp: e.tensor_tensor(
                                    pT[pb][:], pT[pb][:], masks[:, jp * 1024:(jp + 1) * 1024],
                                    ALU.mult), sDVE2, 1)
                                gate = (sDVE2, sDVE2.n)
                            else:
                                gate = (sACT, exp_v)
                                em.wait("vector", sACT, exp_v)
                            ws, wv = tsb_free[pb3]
                            if ws is not None:
                                em.wait("vector", ws, wv)
                            em.op("vector", lambda e, pb=pb, pb3=pb3: e.tensor_tensor(
                                tsb[pb3][:], pT[pb][:, 0:512], pT[pb][:, 512:1024],
                                ALU.add), sDVE2, 1)
                            pendL.append((pb3, sDVE2.n, pr == 0, pr == npair - 1))
                            pend.append((pr, pb, ki0, gate))
                            if len(pend) > 2:
                                emit_pv(*pend.pop(0))
                            if len(pendL) > 4:
                                emit_l(*pendL.pop(0))
                        while pend:
                            emit_pv(*pend.pop(0))
                        while pendL:
                            emit_l(*pendL.pop(0))
                        pv_done = sPE2.n
                        em.wait("vector", sPE2, pv_done)
                        em.op("vector", lambda e: e.tensor_copy(l_row[:], psL[0:1, :]), sDVE, 1)
                        psL_free[0] = (sDVE, sDVE.n)
                        em.wait("tensor", sDVE, sDVE.n)
                        ws, wv = psB_free
                        if ws is not None:
                            em.wait("tensor", ws, wv)
                        em.op("tensor", lambda e: e.matmul(
                            psB[:], onesf[:], l_row[:], start=True, stop=True), sPE, 1)
                        em.wait("scalar", sPE, sPE.n)
                        ws, wv = lbc_free
                        if ws is not None:
                            em.wait("scalar", ws, wv)
                        em.op("scalar", lambda e: e.copy(lbc[:], psB[:]), sACT2, 1)
                        psB_free = (sACT2, sACT2.n)
                        em.wait("vector", sACT2, sACT2.n)
                        em.op("vector", lambda e: e.reciprocal(lbs[:], lbc[:]), sDVE2, 1)
                        em.wait("vector", sDVE2, sDVE2.n)
                        if qg == 0:
                            em.wait("vector", sSt, ast_free[hb])
                        em.op("vector", lambda e, ob=ob, hb=hb, qg=qg: e.tensor_tensor(
                            ast[hb][:, qg * 512:(qg + 1) * 512], psO[ob][:], lbs[:],
                            ALU.mult), sDVE2, 1)
                        lbc_free = (sDVE2, sDVE2.n)
                        psO_free[ob] = (sDVE2, sDVE2.n)
                    em.wait("sync", sDVE2, sDVE2.n)
                    em.op("sync", lambda e, hb=hb, h=h: e.dma_start(
                        attn_hbm.ap()[h * 128:(h + 1) * 128, :], ast[hb][:]), sSt, 16)
                    ast_free[hb] = sSt.n
                    if h + 2 < NH:
                        h_load.append(head_load(h + 2, sPE2.n))
                em.flush("p1")

        if "p2" in PHASES:
            p2 = ExitStack()
            with p2:
                wout_bf = sbt("p2w", [128, KC * D], BF16, p2)
                probs8 = sbt("probs8", [8, S], F32, p2)
                asl = [sbt(f"p2a{i}", [128, KC * 512], BF16, p2) for i in range(2)]
                x_t = [sbt(f"p2x{i}", [128, D], F32, p2) for i in range(2)]
                x2_t = [sbt(f"p2y{i}", [128, D], F32, p2) for i in range(2)]
                stt = [sbt(f"p2s{i}", [128, 16], F32, p2) for i in range(2)]
                sq_scr = sbt("p2sq", [128, D], F32, p2)
                xg_tm = [sbt(f"p2n{i}", [128, D], BF16, p2) for i in range(2)]
                xg_st = [sbt(f"p2g{i}", [128, KC * 512], BF16, p2) for i in range(2)]
                g8 = [sbt(f"p2r{i}", [128, 48], F32, p2) for i in range(2)]
                r_coll = sbt("p2rc", [128, 32], F32, p2)
                rT = sbt("p2rt", [32, 128], F32, p2)
                ppad = [sbt(f"p2pp{i}", [128, 128], F32, p2) for i in range(2)]
                psW = [psumt(f"p2pW{i}", [128, 512], F32, p2) for i in range(2)]
                ptb = [psumt(f"p2pT{i}", [128, 1024], BF16, p2) for i in range(2)]
                psR = psumt("p2pR", [128, 512], F32, p2)
                ptf = psumt("p2pF", [128, 512], F32, p2)

                em.op("gpsimd", lambda e: e.dma_start(
                    wout_bf[:].rearrange("p (c n) -> p c n", c=KC),
                    woutT.ap().rearrange("(c p) n -> p c n", p=128)), sW, 16)
                wout_v = sW.n
                for i in range(2):
                    em.op("vector", lambda e, i=i: e.memset(ppad[i][:], 0.0))
                    em.op("vector", lambda e, i=i: e.memset(ppad[i][:, 7:8], 1.0), sDVE, 1)

                def rcol(b, i, w=7):
                    return g8[b][:, i:i + w]

                asl_free = [0, 0]
                xt_free = [0, 0]
                sq_mark = {}
                x2w_free = [0, 0]
                x2n_free = [0, 0]
                x2sq_free = [0, 0]
                r_mark2 = {}
                xg_free = [0, 0]
                psW_free = {0: (None, 0), 1: (None, 0)}
                pt_free = {0: (None, 0), 1: (None, 0)}
                ppad_free = [(None, 0), (None, 0)]
                psR_free = (None, 0)
                ptf_free = (None, 0)
                asl_v_tab = {}

                def load_asl(sl):
                    sb_ = sl % 2
                    em.wait("sync", sDVE, asl_free[sb_])
                    em.op("sync", lambda e, sb_=sb_, sl=sl: e.dma_start(
                        asl[sb_][:].rearrange("p (c n) -> p c n", c=KC),
                        dap(attn_hbm, sl * 512, [[S, 128], [128 * S, KC], [1, 512]])), sL, 16)
                    asl_v_tab[sl] = sL.n

                def emit_tail(ti, tb, t4, gb_):
                    nonlocal psR_free, ptf_free
                    if True:
                        em.wait("tensor", sDVE2, x2n_tab[ti])
                        if t4 == 0:
                            em.wait("tensor", sSt, xg_free[gb_])
                        for kb in range(4):
                            pb = kb % 2
                            ws, wv = pt_free[pb]
                            if ws is not None:
                                em.wait("tensor", ws, wv)
                            for j in range(4):
                                kc = kb * 4 + j
                                em.op("tensor", lambda e, pb=pb, tb=tb, kc=kc, j=j: e.transpose(
                                    ptb[pb][:, j * 128:(j + 1) * 128],
                                    xg_tm[tb][:, kc * 128:(kc + 1) * 128],
                                    identb[:]), sPE if j == 3 else None, 1)
                            em.wait("vector", sPE, sPE.n)
                            em.op("vector", lambda e, pb=pb, sb_=gb_, kb=kb, t4=t4: e.tensor_copy(
                                dap(xg_st[sb_], kb * 4 * 512 + t4 * 128,
                                    [[KC * 512, 128], [512, 4], [1, 128]]),
                                ptb[pb][:, 0:512]), sDVE, 1)
                            pt_free[pb] = (sDVE, sDVE.n)
                        tr_v = sDVE.n
                        em.wait("tensor", sDVE, tr_v)
                        if psR_free[0] is not None:
                            em.wait("tensor", psR_free[0], psR_free[1])
                        for kc in range(KC):
                            em.op("tensor", lambda e, sb_=gb_, kc=kc, t4=t4: e.matmul(
                                psR[:, 0:7],
                                xg_st[sb_][:, kc * 512 + t4 * 128: kc * 512 + (t4 + 1) * 128],
                                wr_sb[:, kc * 7:(kc + 1) * 7],
                                start=(kc == 0), stop=(kc == KC - 1)),
                                sPE if kc == KC - 1 else None, 1)
                        gmm_v = sPE.n
                        tb2 = ti % 2
                        em.wait("vector", sPE, gmm_v)
                        em.wait("vector", sDVE2, r_mark2[ti])
                        em.op("vector", lambda e, tb2=tb2, ti=ti: e.tensor_scalar(
                            rcol(tb2, 0), psR[:, 0:7], r_coll[:, ti:ti + 1], None,
                            ALU.mult), sDVE, 1)
                        psR_free = (sDVE, sDVE.n)
                        em.wait("vector", sDVE, sDVE.n)
                        em.op("vector", lambda e, tb2=tb2: e.tensor_tensor(
                            rcol(tb2, 0), rcol(tb2, 0), br_bc[:], ALU.add), sDVE, 1)
                        em.wait("vector", sDVE, sDVE.n)
                        em.op("vector", lambda e, tb2=tb2: e.tensor_tensor(
                            rcol(tb2, 8), rcol(tb2, 0), gb_bc[:], ALU.add), sDVE, 1)
                        em.wait("vector", sDVE, sDVE.n)
                        em.op("vector", lambda e, tb2=tb2: e.tensor_reduce(
                            g8[tb2][:, 16:17], rcol(tb2, 8), AX.X, ALU.max), sDVE, 1)
                        em.wait("vector", sDVE, sDVE.n)
                        em.op("vector", lambda e, tb2=tb2: e.tensor_scalar(
                            rcol(tb2, 24), rcol(tb2, 8), g8[tb2][:, 16:17], -BIGL,
                            ALU.is_equal, ALU.mult), sDVE, 1)
                        em.wait("vector", sDVE, sDVE.n)
                        em.op("vector", lambda e, tb2=tb2: e.tensor_tensor(
                            rcol(tb2, 24), rcol(tb2, 24), rcol(tb2, 8), ALU.add), sDVE, 1)
                        em.wait("vector", sDVE, sDVE.n)
                        em.op("vector", lambda e, tb2=tb2: e.tensor_reduce(
                            g8[tb2][:, 17:18], rcol(tb2, 24), AX.X, ALU.max), sDVE, 1)
                        em.wait("vector", sDVE, sDVE.n)
                        em.op("vector", lambda e, tb2=tb2: e.tensor_scalar(
                            rcol(tb2, 32), rcol(tb2, 8), g8[tb2][:, 17:18], BIGL,
                            ALU.is_ge, ALU.mult), sDVE, 1)
                        em.wait("vector", sDVE, sDVE.n)
                        em.op("vector", lambda e, tb2=tb2: e.tensor_tensor(
                            rcol(tb2, 32), rcol(tb2, 32), rcol(tb2, 0), ALU.add), sDVE, 1)
                        em.wait("vector", sDVE, sDVE.n)
                        em.op("vector", lambda e, tb2=tb2: e.tensor_reduce(
                            g8[tb2][:, 18:19], rcol(tb2, 32), AX.X, ALU.max,
                            negate=True), sDVE, 1)
                        em.wait("scalar", sDVE, sDVE.n)
                        em.op("scalar", lambda e, tb2=tb2: e.activation(
                            rcol(tb2, 40), rcol(tb2, 32), AF.Exp,
                            bias=g8[tb2][:, 18:19]), sACT2, 1)
                        em.wait("vector", sACT2, sACT2.n)
                        em.op("vector", lambda e, tb2=tb2: e.tensor_reduce(
                            g8[tb2][:, 19:20], rcol(tb2, 40), AX.X, ALU.add), sDVE, 1)
                        em.wait("vector", sDVE, sDVE.n)
                        em.op("vector", lambda e, tb2=tb2: e.reciprocal(
                            g8[tb2][:, 20:21], g8[tb2][:, 19:20]), sDVE, 1)
                        em.wait("vector", sDVE, sDVE.n)
                        ws, wv = ppad_free[tb2]
                        if ws is not None:
                            em.wait("vector", ws, wv)
                        em.op("vector", lambda e, tb2=tb2: e.tensor_scalar(
                            ppad[tb2][:, 0:7], rcol(tb2, 40), g8[tb2][:, 20:21], None,
                            ALU.mult), sDVE, 1)
                        pr_v = sDVE.n
                        em.wait("tensor", sDVE, pr_v)
                        if ptf_free[0] is not None:
                            em.wait("tensor", ptf_free[0], ptf_free[1])
                        em.op("tensor", lambda e, tb2=tb2: e.transpose(
                            ptf[:, 0:128], ppad[tb2][:], identf[:]), sPE, 1)
                        ppad_free[tb2] = (sPE, sPE.n)
                        em.wait("vector", sPE, sPE.n)
                        em.op("vector", lambda e, ti=ti: e.tensor_copy(
                            probs8[0:8, ti * 128:(ti + 1) * 128], ptf[0:8, 0:128]), sDVE, 1)
                        ptf_free = (sDVE, sDVE.n)
                    if t4 == 3:
                        sl = ti // 4
                        em.wait("sync", sDVE, sDVE.n)
                        em.op("sync", lambda e, sb_=gb_, sl=sl: e.dma_start(
                            dap(xg_hbm, sl * 512, [[S, 128], [128 * S, KC], [1, 512]]),
                            xg_st[sb_][:].rearrange("p (c n) -> p c n", c=KC)), sSt, 16)
                        xg_free[gb_] = sSt.n
                        xmark["xg_done"] = sSt.n

                pend_tail = []
                x2n_tab = {}
                load_asl(0)
                for sl in range(8):
                    sb_ = sl % 2
                    gb_ = sl % 2
                    asl_v = asl_v_tab[sl]
                    for t4 in range(4):
                        ti = sl * 4 + t4
                        tb = ti % 2
                        em.wait("sync", sDVE, xt_free[tb])
                        em.op("sync", lambda e, tb=tb, ti=ti: e.dma_start(
                            x_t[tb][:], x_in.ap()[ti * 128:(ti + 1) * 128, :]), sL, 16)
                        x_v = sL.n
                        if t4 == 0 and sl + 1 < 8:
                            load_asl(sl + 1)
                        for oc in range(4):
                            pb = oc % 2
                            if t4 == 0 and oc == 0:
                                em.wait("tensor", sL, asl_v)
                                if sl == 0:
                                    em.wait("tensor", sW, wout_v)
                            ws, wv = psW_free[pb]
                            if ws is not None:
                                em.wait("tensor", ws, wv)
                            for kc in range(KC):
                                em.op("tensor", lambda e, pb=pb, sb_=sb_, kc=kc, t4=t4, oc=oc: e.matmul(
                                    psW[pb][:],
                                    asl[sb_][:, kc * 512 + t4 * 128: kc * 512 + (t4 + 1) * 128],
                                    wout_bf[:, kc * D + oc * 512: kc * D + oc * 512 + 512],
                                    start=(kc == 0), stop=(kc == KC - 1)),
                                    sPE if kc == KC - 1 else None, 1)
                            em.wait("vector", sPE, sPE.n)
                            em.wait("vector", sL, x_v)
                            if oc == 0:
                                em.wait("vector", sSt, x2w_free[tb])
                                em.wait("vector", sDVE2, x2n_free[tb])
                                em.wait("vector", sACT, x2sq_free[tb])
                            em.op("vector", lambda e, tb=tb, pb=pb, oc=oc: e.tensor_tensor(
                                x2_t[tb][:, oc * 512:(oc + 1) * 512], psW[pb][:],
                                x_t[tb][:, oc * 512:(oc + 1) * 512], ALU.add), sDVE, 1)
                            psW_free[pb] = (sDVE, sDVE.n)
                        x2_v = sDVE.n
                        xt_free[tb] = x2_v
                        if t4 == 3:
                            asl_free[sb_] = x2_v
                        em.wait("sync", sDVE, x2_v)
                        em.op("sync", lambda e, tb=tb, ti=ti: e.dma_start(
                            x2_hbm.ap()[ti * 128:(ti + 1) * 128, :], x2_t[tb][:]), sSt, 16)
                        x2w_free[tb] = sSt.n
                        em.wait("vector", sDVE, x2_v)
                        em.op("vector", lambda e, tb=tb: e.reduce_sum(
                            stt[tb][:, 0:1], x2_t[tb][:], AX.X), sDVE2, 1)
                        em.wait("vector", sDVE2, sDVE2.n)
                        em.op("vector", lambda e, tb=tb: e.tensor_scalar(
                            stt[tb][:, 2:3], stt[tb][:, 0:1], 1.0 / D, None, ALU.mult), sDVE2, 1)
                        em.wait("vector", sDVE2, sDVE2.n)
                        em.op("vector", lambda e, tb=tb: e.tensor_scalar(
                            xg_tm[tb][:], x2_t[tb][:], stt[tb][:, 2:3], None,
                            ALU.subtract), sDVE2, 1)
                        x2n_free[tb] = sDVE2.n
                        x2n_tab[ti] = sDVE2.n
                        em.wait("scalar", sDVE, x2_v)
                        if ti >= 1:
                            em.wait("scalar", sDVE2, sq_mark[ti - 1])
                        em.op("scalar", lambda e, tb=tb: e.activation(
                            sq_scr[:], x2_t[tb][:], AF.Square), sACT, 1)
                        x2sq_free[tb] = sACT.n
                        em.wait("vector", sACT, sACT.n)
                        em.op("vector", lambda e, tb=tb: e.reduce_sum(
                            stt[tb][:, 1:2], sq_scr[:], AX.X), sDVE2, 1)
                        sq_mark[ti] = sDVE2.n
                        em.wait("vector", sDVE2, sDVE2.n)
                        em.op("vector", lambda e, tb=tb: e.tensor_scalar(
                            stt[tb][:, 4:5], stt[tb][:, 1:2], 1.0 / D, None, ALU.mult), sDVE2, 1)
                        em.op("vector", lambda e, tb=tb: e.tensor_tensor(
                            stt[tb][:, 3:4], stt[tb][:, 2:3], stt[tb][:, 2:3], ALU.mult), sDVE2, 1)
                        em.wait("vector", sDVE2, sDVE2.n)
                        em.op("vector", lambda e, tb=tb: e.tensor_sub(
                            stt[tb][:, 5:6], stt[tb][:, 4:5], stt[tb][:, 3:4]), sDVE2, 1)
                        em.wait("scalar", sDVE2, sDVE2.n)
                        em.op("scalar", lambda e, tb=tb: e.activation(
                            stt[tb][:, 6:7], stt[tb][:, 5:6], AF.Sqrt, bias=epsb[:]), sACT, 1)
                        em.wait("vector", sACT, sACT.n)
                        em.op("vector", lambda e, tb=tb, ti=ti: e.reciprocal(
                            r_coll[:, ti:ti + 1], stt[tb][:, 6:7]), sDVE2, 1)
                        r_mark2[ti] = sDVE2.n
                        pend_tail.append((ti, tb, t4, gb_))
                        if len(pend_tail) > 1:
                            emit_tail(*pend_tail.pop(0))
                while pend_tail:
                    emit_tail(*pend_tail.pop(0))
                em.wait("tensor", sDVE2, r_mark2[31])
                if ptf_free[0] is not None:
                    em.wait("tensor", ptf_free[0], ptf_free[1])
                em.op("tensor", lambda e: e.transpose(
                    ptf[0:32, 0:128], r_coll[:], identf[:]), sPE, 1)
                em.wait("vector", sPE, sPE.n)
                em.op("vector", lambda e: e.tensor_copy(rT[:], ptf[0:32, 0:128]), sDVE, 1)
                em.wait("sync", sDVE, sDVE.n)
                em.op("sync", lambda e: e.dma_start(r_hbm.ap(), rT[:]), sSt, 16)
                em.wait("gpsimd", sDVE, sDVE.n)
                em.op("gpsimd", lambda e: e.dma_start(prow_hbm.ap(), probs8[:]), sW, 16)
                xmark["prow_w"] = sW.n
                em.wait("sync", sDVE, sDVE.n)
                em.op("sync", lambda e: e.dma_start(probs8f_hbm.ap(), probs8[:]), sSt, 16)
                em.wait("sync", sSt, sSt.n)
                em.wait("gpsimd", sW, sW.n)
                em.flush("p2")

        if "p3" in PHASES:
            p3 = ExitStack()
            with p3:
                xg_fm = sbt("p3xg", [128, KC * S], BF16, p3)
                wsl = [sbt(f"p3w{i}", [128, KC * 256], BF16, p3) for i in range(2)]
                ust = [sbt(f"p3u{i}", [128, 4 * 512], F32, p3) for i in range(2)]
                r_row = sbt("p3rr", [1, S], F32, p3)
                r_bc = sbt("p3rb", [128, S], F32, p3)
                psU = [psumt(f"p3p{i}", [128, 512], F32, p3) for i in range(2)]

                em.op("sync", lambda e: e.dma_start(
                    r_row[:], dap(r_hbm, 0, [[S, 1], [1, S]])), sL, 16)
                rr_v = sL.n
                em.op("sync", lambda e: e.dma_start(
                    xg_fm[:].rearrange("p (c n) -> p c n", c=KC),
                    dap(xg_hbm, 0, [[S, 128], [128 * S, KC], [1, S]])), sL, 16)
                xg_v = sL.n
                em.wait("tensor", sL, rr_v)
                cp_tab = {}
                for j in range(8):
                    if j >= 2:
                        em.wait("tensor", sDVE, cp_tab[j - 2])
                    em.op("tensor", lambda e, j=j: e.matmul(
                        psU[j % 2][:], onesf[:], r_row[0:1, j * 512:(j + 1) * 512],
                        start=True, stop=True), sPE, 1)
                    em.wait("vector", sPE, sPE.n)
                    em.op("vector", lambda e, j=j: e.tensor_copy(
                        r_bc[:, j * 512:(j + 1) * 512], psU[j % 2][:]), sDVE, 1)
                    cp_tab[j] = sDVE.n
                em.wait("tensor", sDVE, cp_tab[7])
                psU_free = {0: (None, 0), 1: (None, 0)}
                ust_free = [0, 0]
                wslab_mark = {}
                nslab = 0
                for e_ in range(NE):
                    for uc2 in range(8):
                        wb = nslab % 2
                        if nslab >= 2:
                            em.wait("gpsimd", sPE2, wslab_mark[nslab - 2])
                        em.op("gpsimd", lambda e, e_=e_, uc2=uc2, wb=wb: e.dma_start(
                            wsl[wb][:].rearrange("p (c n) -> p c n", c=KC),
                            dap(wup_in, e_ * D * D + uc2 * 256,
                                [[D, 128], [128 * D, KC], [1, 256]])), sW, 16)
                        w_v = sW.n
                        for uch in range(2):
                            uc = uc2 * 2 + uch
                            for tw in range(8):
                                pb = tw % 2
                                ub = (nslab * 4 + uch * 2 + tw // 4) % 2
                                if tw % 4 == 0:
                                    em.wait("vector", sSt, ust_free[ub])
                                if tw == 0:
                                    em.wait("tensor", sW, w_v)
                                    if nslab == 0 and uch == 0:
                                        em.wait("tensor", sL, xg_v)
                                ws, wv = psU_free[pb]
                                if ws is not None:
                                    em.wait("tensor", ws, wv)
                                for kc in range(KC):
                                    em.op("tensor", lambda e, pb=pb, wb=wb, kc=kc, uch=uch, tw=tw: e.matmul(
                                        psU[pb][:],
                                        wsl[wb][:, kc * 256 + uch * 128: kc * 256 + (uch + 1) * 128],
                                        xg_fm[:, kc * S + tw * 512: kc * S + tw * 512 + 512],
                                        start=(kc == 0), stop=(kc == KC - 1)),
                                        sPE2 if kc == KC - 1 else None, 1)
                                em.wait("vector", sPE2, sPE2.n)
                                em.op("vector", lambda e, ub=ub, tw=tw % 4, gtw=tw, pb=pb: e.tensor_tensor(
                                    ust[ub][:, tw * 512:(tw + 1) * 512], psU[pb][:],
                                    r_bc[:, gtw * 512:(gtw + 1) * 512], ALU.mult), sDVE, 1)
                                psU_free[pb] = (sDVE, sDVE.n)
                                em.wait("vector", sDVE, sDVE.n)
                                em.op("vector", lambda e, ub=ub, tw=tw % 4, uc=uc, e_=e_: e.tensor_scalar(
                                    ust[ub][:, tw * 512:(tw + 1) * 512],
                                    ust[ub][:, tw * 512:(tw + 1) * 512],
                                    bup_sb[:, uc * NE + e_: uc * NE + e_ + 1], None,
                                    ALU.add), sDVE, 1)
                                if tw % 4 == 3:
                                    em.wait("sync", sDVE, sDVE.n)
                                    em.op("sync", lambda e, ub=ub, e_=e_, uc=uc, th=tw // 4: e.dma_start(
                                        dap(u_hbm, (e_ * D + uc * 128) * S + th * 2048,
                                            [[S, 128], [1, 2048]]),
                                        ust[ub][:]), sSt, 16)
                                    ust_free[ub] = sSt.n
                        wslab_mark[nslab] = sPE2.n
                        nslab += 1
                em.flush("p3")

        if "p4" in PHASES:
            p4 = ExitStack()
            with p4:
                wg = [sbt(f"p4w{i}", [128, KC * D], BF16, p4) for i in range(2)]
                usl = [sbt(f"p4u{i}", [128, KC * 256], F32, p4) for i in range(2)]
                ubf = [sbt(f"p4ub{i}", [128, KC * 256], BF16, p4) for i in range(2)]
                sgt = [sbt(f"p4sg{i}", [128, 512], F32, p4) for i in range(2)]
                gst = [sbt(f"p4g{i}", [128, 8 * 256], BF16, p4) for i in range(2)]
                prow = sbt("p4p", [1, S], BF16, p4)
                pbc = sbt("p4pb", [128, 256], F32, p4)
                psG = [psumt(f"p4p{i}", [128, 512], F32, p4) for i in range(2)]
                psP = psumt("p4pp", [128, 512], F32, p4)

                wg_free = [0, 0]
                usl_free = [0, 0]
                ubf_free = [0, 0]
                gst_free = [0, 0]
                sgt_free = [(None, 0), (None, 0)]
                psG_free = {0: (None, 0), 1: (None, 0)}
                psP_free = (None, 0)
                pbc_free = 0

                ub_v_tab = {}

                def load_usl(e_, tw):
                    twb = tw % 2
                    em.wait("sync", sDVE, usl_free[twb])
                    em.op("sync", lambda e, e_=e_, tw=tw, twb=twb: e.dma_start(
                        usl[twb][:].rearrange("p (c n) -> p c n", c=KC),
                        dap(u_hbm, e_ * D * S + tw * 256,
                            [[S, 128], [128 * S, KC], [1, 256]])), sL, 16)
                    return sL.n

                def load_ubf(e_, tw):
                    tb3 = (e_ * 16 + tw) % 2
                    em.wait("gpsimd", sPE2, ubf_free[tb3])
                    em.op("gpsimd", lambda e, e_=e_, tw=tw, tb3=tb3: e.dma_start(
                        ubf[tb3][:].rearrange("p (c n) -> p c n", c=KC),
                        dap(u_hbm, e_ * D * S + tw * 256,
                            [[S, 128], [128 * S, KC], [1, 256]])), sL, 16)
                    ub_v_tab[(e_, tw)] = sL.n

                prow_v_tab = {}
                p4_sPE_base = sPE.n

                def load_prow(e_):
                    if e_ == 0:
                        em.wait("scalar", sW, xmark["prow_w"])
                    else:
                        em.wait("scalar", sPE, p4_sPE_base + 16 * e_)
                    em.op("scalar", lambda e, e_=e_: e.dma_start(
                        prow[:], prow_hbm.ap()[e_:e_ + 1, :]), sX, 16)
                    prow_v_tab[e_] = sX.n

                wg_v_tab = {}

                def load_wg_slice(e_, q):
                    eb = e_ % 2
                    if q == 0:
                        em.wait("gpsimd", sPE2, wg_free[eb])
                    em.op("gpsimd", lambda e, e_=e_, eb=eb, q=q: e.dma_start(
                        wg[eb][:].rearrange("p (c n) -> p c n", c=KC)
                        [:, :, q * 256:(q + 1) * 256],
                        dap(wgate_in, e_ * D * D + q * 256,
                            [[D, 128], [128 * D, KC], [1, 256]])), sW, 16)
                    wg_v_tab[e_] = sW.n

                u_v_next = load_usl(0, 0)
                load_ubf(0, 0)
                load_prow(0)
                for q in range(8):
                    load_wg_slice(0, q)
                for e_ in range(NE):
                    eb = e_ % 2
                    wg_v = wg_v_tab[e_]
                    prow_v = prow_v_tab[e_]
                    for tw in range(16):
                        twb = tw % 2
                        tb3 = (e_ * 16 + tw) % 2
                        u_v = u_v_next
                        if tw < 15:
                            u_v_next = load_usl(e_, tw + 1)
                        elif e_ < NE - 1:
                            u_v_next = load_usl(e_ + 1, 0)
                        if tw + 1 < 16:
                            load_ubf(e_, tw + 1)
                        elif e_ < NE - 1:
                            load_ubf(e_ + 1, 0)
                        ub_v = ub_v_tab[(e_, tw)]
                        _wgq = {2: 0, 3: 1, 5: 2, 6: 3, 8: 4, 9: 5, 11: 6, 12: 7}
                        if tw in _wgq and e_ + 1 < NE:
                            load_wg_slice(e_ + 1, _wgq[tw])
                        em.wait("tensor", sX, prow_v)
                        if psP_free[0] is not None:
                            em.wait("tensor", psP_free[0], psP_free[1])
                        em.op("tensor", lambda e, tw=tw: e.matmul(
                            psP[:, 0:256], onesb[0:1, :], prow[0:1, tw * 256:(tw + 1) * 256],
                            start=True, stop=True), sPE, 1)
                        em.wait("scalar", sPE, sPE.n)
                        em.wait("scalar", sDVE, pbc_free)
                        em.op("scalar", lambda e: e.copy(pbc[:], psP[:, 0:256]), sACT2, 1)
                        psP_free = (sACT2, sACT2.n)
                        pbc_v = sACT2.n
                        if tw == 0:
                            em.wait("tensor", sW, wg_v)
                        em.wait("tensor", sL, ub_v)
                        for gp in range(8):
                            gc0 = 2 * gp
                            pb = gp % 2
                            ws, wv = psG_free[pb]
                            if ws is not None:
                                em.wait("tensor", ws, wv)
                            for gph in range(2):
                                gc = gc0 + gph
                                for kc in range(KC):
                                    em.op("tensor", lambda e, pb=pb, gph=gph, gc=gc, kc=kc, eb=eb, twb=tb3: e.matmul(
                                        psG[pb][:, gph * 256:(gph + 1) * 256],
                                        wg[eb][:, kc * D + gc * 128: kc * D + (gc + 1) * 128],
                                        ubf[twb][:, kc * 256:(kc + 1) * 256],
                                        start=(kc == 0), stop=(kc == KC - 1)),
                                        sPE2 if (kc == KC - 1 and gph == 1) else None, 1)
                            mm_v = sPE2.n
                            em.wait("scalar", sPE2, mm_v)
                            ws, wv = sgt_free[pb]
                            if ws is not None:
                                em.wait("scalar", ws, wv)
                            for gph in range(2):
                                gc = gc0 + gph
                                em.op("scalar", lambda e, pb=pb, gph=gph, gc=gc, e_=e_: e.activation(
                                    sgt[pb][:, gph * 256:(gph + 1) * 256],
                                    psG[pb][:, gph * 256:(gph + 1) * 256], AF.Silu,
                                    bias=bgate_sb[:, gc * NE + e_: gc * NE + e_ + 1]),
                                    sACT, 1)
                            psG_free[pb] = (sACT, sACT.n)
                            em.wait("vector", sACT, sACT.n)
                            if gp == 0:
                                em.wait("vector", sL, u_v)
                                em.wait("vector", sACT2, pbc_v)
                            hb4 = gp // 4
                            if gp % 4 == 0:
                                em.wait("vector", sSt, gst_free[hb4])
                            em.op("vector", lambda e, pb=pb, gc0=gc0, twb=twb: e.tensor_tensor(
                                sgt[pb][:], sgt[pb][:],
                                usl[twb][:, gc0 * 256: gc0 * 256 + 512], ALU.mult), sDVE, 1)
                            em.wait("vector", sDVE, sDVE.n)
                            pbc2 = dap(pbc, 0, [[256, 128], [0, 2], [1, 256]])
                            em.op("vector", lambda e, pb=pb, hb4=hb4, lc0=gc0 % 8, p2_=pbc2: e.tensor_tensor(
                                gst[hb4][:].rearrange("p (c n) -> p c n", c=8)
                                [:, lc0:lc0 + 2, :], sgt[pb][:].rearrange(
                                    "p (c n) -> p c n", c=2), p2_, ALU.mult), sDVE, 1)
                            sgt_free[pb] = (sDVE, sDVE.n)
                            if gp % 4 == 3:
                                em.wait("sync", sDVE, sDVE.n)
                                em.op("sync", lambda e, e_=e_, tw=tw, hb4=hb4: e.dma_start(
                                    dap(g_hbm, e_ * D * S + hb4 * 8 * 128 * S + tw * 256,
                                        [[S, 128], [128 * S, 8], [1, 256]]),
                                    gst[hb4][:].rearrange("p (c n) -> p c n", c=8)), sSt, 16)
                                gst_free[hb4] = sSt.n
                        usl_free[twb] = sDVE.n
                        ubf_free[tb3] = sPE2.n
                        pbc_free = sDVE.n
                    wg_free[eb] = sPE2.n
                    if e_ + 1 < NE:
                        load_prow(e_ + 1)
                em.flush("p4")


        if "p5" in PHASES:
            p5 = ExitStack()
            with p5:
                bdmat_sb = sbt("bdmat_sb", [8, D], F32, p5)
                probs8 = sbt("probs8_5", [8, S], F32, p5)
                oacc = sbt("p5o", [128, KC * 1024], F32, p5)
                gsl = [sbt(f"p5g{i}", [128, KC * 1024], BF16, p5) for i in range(2)]
                wdl = [sbt(f"p5w{i}", [128, KC * 128], BF16, p5) for i in range(4)]
                x2t = [sbt(f"p5x{i}", [128, D], F32, p5) for i in range(2)]
                ost = [sbt(f"p5s{i}", [128, D], F32, p5) for i in range(2)]
                psD = [psumt(f"p5p{i}", [128, 512], F32, p5) for i in range(4)]
                ptf5 = [psumt(f"p5t{i}", [128, 512], F32, p5) for i in range(2)]

                em.op("sync", lambda e: e.dma_start(bdmat_sb[:], bdmat_in.ap()), sL, 16)
                em.op("sync", lambda e: e.dma_start(probs8[:], probs8f_hbm.ap()), sL, 16)
                bd_v = sL.n
                gsl_free = [0, 0]
                wdl_mark = {}
                psD_free = {0: (None, 0), 1: (None, 0), 2: (None, 0), 3: (None, 0)}
                ptf_free = {0: (None, 0), 1: (None, 0)}
                x2_free = [0, 0]
                ost_free = [0, 0]
                nwd = 0
                for grp in range(4):
                    for e_ in range(NE):
                        gb = e_ % 2
                        em.wait("sync", sPE2, gsl_free[gb])
                        em.op("sync", lambda e, e_=e_, grp=grp, gb=gb: e.dma_start(
                            gsl[gb][:].rearrange("p (c n) -> p c n", c=KC),
                            dap(g_hbm, e_ * D * S + grp * 1024,
                                [[S, 128], [128 * S, KC], [1, 1024]])), sL, 16)
                        g_v = sL.n
                        for cc in range(KC):
                            wb = nwd % 4
                            wq = "gpsimd"
                            if nwd >= 4:
                                em.wait(wq, sPE2, wdl_mark[nwd - 4])
                            em.op(wq, lambda e, e_=e_, cc=cc, wb=wb: e.dma_start(
                                wdl[wb][:].rearrange("p (c n) -> p c n", c=KC),
                                dap(wdown_in, e_ * D * D + cc * 128,
                                    [[D, 128], [128 * D, KC], [1, 128]])), sW, 16)
                            w_v = sW.n
                            pb0 = (cc % 2) * 2
                            em.wait("tensor", sW, w_v)
                            if cc == 0:
                                em.wait("tensor", sL, g_v)
                                if grp == 0 and e_ == 0:
                                    em.wait("tensor", sL, bd_v)
                            for sw in range(2):
                                ws, wv = psD_free[pb0 + sw]
                                if ws is not None:
                                    em.wait("tensor", ws, wv)
                            for kc in range(KC):
                                for sw in range(2):
                                    last = (kc == KC - 1) and (e_ != 0)
                                    em.op("tensor", lambda e, pb=pb0 + sw, wb=wb, gb=gb, kc=kc, sw=sw, last=last: e.matmul(
                                        psD[pb][:],
                                        wdl[wb][:, kc * 128:(kc + 1) * 128],
                                        gsl[gb][:, kc * 1024 + sw * 512: kc * 1024 + (sw + 1) * 512],
                                        start=(kc == 0), stop=last),
                                        sPE2 if last else None, 1)
                            if e_ == 0:
                                for sw in range(2):
                                    em.op("tensor", lambda e, pb=pb0 + sw, cc=cc, grp=grp, sw=sw: e.matmul(
                                        psD[pb][:], bdmat_sb[:, cc * 128:(cc + 1) * 128],
                                        probs8[:, grp * 1024 + sw * 512: grp * 1024 + sw * 512 + 512],
                                        start=False, stop=True), sPE2, 1)
                            mm_v = sPE2.n
                            em.wait("vector", sPE2, mm_v)
                            for sw in range(2):
                                if e_ == 0:
                                    em.op("vector", lambda e, pb=pb0 + sw, cc=cc, sw=sw: e.tensor_copy(
                                        oacc[:, cc * 1024 + sw * 512: cc * 1024 + (sw + 1) * 512],
                                        psD[pb][:]), sDVE, 1)
                                else:
                                    em.op("vector", lambda e, pb=pb0 + sw, cc=cc, sw=sw: e.tensor_tensor(
                                        oacc[:, cc * 1024 + sw * 512: cc * 1024 + (sw + 1) * 512],
                                        oacc[:, cc * 1024 + sw * 512: cc * 1024 + (sw + 1) * 512],
                                        psD[pb][:], ALU.add), sDVE, 1)
                                psD_free[pb0 + sw] = (sDVE, sDVE.n)
                            wdl_mark[nwd] = sPE2.n
                            nwd += 1
                        gsl_free[gb] = sPE2.n
                    acc_done = sDVE.n
                    x2v_tab = {}

                    def load_x2(tj):
                        tb = tj % 2
                        em.wait("sync", sDVE2, x2_free[tb])
                        em.op("sync", lambda e, tt=grp * 8 + tj, tb=tb: e.dma_start(
                            x2t[tb][:], x2_hbm.ap()[tt * 128:(tt + 1) * 128, :]), sL, 16)
                        x2v_tab[tj] = sL.n

                    load_x2(0)
                    load_x2(1)
                    for tj in range(8):
                        tt = grp * 8 + tj
                        tb = tj % 2
                        x2_v = x2v_tab[tj]
                        em.wait("vector", sL, x2_v)
                        if tj == 0:
                            em.wait("tensor", sDVE, acc_done)
                        for cg in range(4):
                            pb = cg % 2
                            ws, wv = ptf_free[pb]
                            if ws is not None:
                                em.wait("tensor", ws, wv)
                            for i4 in range(4):
                                cc = cg * 4 + i4
                                em.op("tensor", lambda e, pb=pb, cc=cc, i4=i4, tj=tj: e.transpose(
                                    ptf5[pb][:, i4 * 128:(i4 + 1) * 128],
                                    oacc[:, cc * 1024 + tj * 128: cc * 1024 + (tj + 1) * 128],
                                    identf[:]), sPE if i4 == 3 else None, 1)
                            em.wait("vector", sPE, sPE.n)
                            if cg == 0:
                                em.wait("vector", sSt, ost_free[tb])
                            em.op("vector", lambda e, pb=pb, cg=cg, tb=tb: e.tensor_tensor(
                                ost[tb][:, cg * 512:(cg + 1) * 512], ptf5[pb][:],
                                x2t[tb][:, cg * 512:(cg + 1) * 512], ALU.add), sDVE2, 1)
                            ptf_free[pb] = (sDVE2, sDVE2.n)
                        em.wait("sync", sDVE2, sDVE2.n)
                        em.op("sync", lambda e, tt=tt, tb=tb: e.dma_start(
                            x_out.ap()[tt * 128:(tt + 1) * 128, :], ost[tb][:]), sSt, 16)
                        ost_free[tb] = sSt.n
                        x2_free[tb] = sDVE2.n
                        if tj + 2 < 8:
                            load_x2(tj + 2)
                em.wait("sync", sSt, sSt.n)
                em.wait("gpsimd", sW, sW.n)
                em.flush("p5")
        else:
            em.wait("sync", sSt, sSt.n)
            em.wait("gpsimd", sW, sW.n)
            em.flush("tail")

    return nc


def _prep_inputs(L0_attn_w_qkv, L0_attn_w_out, L0_ffn_Wr, L0_ffn_br,
                 L0_ffn_gate_bias, L0_ffn_Wup, L0_ffn_bup, L0_ffn_Wgate,
                 L0_ffn_bgate, L0_ffn_Wdown, L0_ffn_bdown, L0_ffn_Wsu,
                 L0_ffn_bsu, L0_ffn_Wsg, L0_ffn_bsg, L0_ffn_Wsd, L0_ffn_bsd):
    f32 = np.float32
    bf16 = ml_dtypes.bfloat16

    def src_feature(kind, h, d):
        m, j = divmod(h, 4)
        off = {"q": 0, "v": 512, "k": 1024}[kind]
        return m * 1536 + off + j * 128 + d

    def rot_src(dp):
        if dp < 32:
            return 2 * dp
        if dp < 64:
            return 2 * (dp - 32) + 1
        return dp

    perm = np.empty(3 * D, dtype=np.int64)
    for h in range(NH):
        for dp in range(HD):
            perm[h * HD + dp] = src_feature("q", h, rot_src(dp))
            perm[D + h * HD + dp] = src_feature("k", h, rot_src(dp))
            perm[2 * D + h * HD + dp] = src_feature("v", h, dp)
    wqkvT = np.ascontiguousarray(np.asarray(L0_attn_w_qkv)[perm, :].T, dtype=f32)
    woutT = np.ascontiguousarray(np.asarray(L0_attn_w_out).T, dtype=f32)

    wup = np.ascontiguousarray(np.concatenate(
        [np.asarray(L0_ffn_Wup), np.asarray(L0_ffn_Wsu)[None]], axis=0), dtype=f32)
    wgate = np.ascontiguousarray(np.concatenate(
        [np.asarray(L0_ffn_Wgate), np.asarray(L0_ffn_Wsg)[None]], axis=0), dtype=f32)
    wdown = np.ascontiguousarray(np.concatenate(
        [np.asarray(L0_ffn_Wdown), np.asarray(L0_ffn_Wsd)[None]], axis=0), dtype=f32)
    bup8 = np.concatenate([np.asarray(L0_ffn_bup),
                           np.asarray(L0_ffn_bsu)[None]], axis=0).astype(f32)
    bgate8 = np.concatenate([np.asarray(L0_ffn_bgate),
                             np.asarray(L0_ffn_bsg)[None]], axis=0).astype(f32)
    bdmat = np.concatenate([np.asarray(L0_ffn_bdown),
                            np.asarray(L0_ffn_bsd)[None]], axis=0).astype(f32)
    bup_t = np.ascontiguousarray(bup8.T.reshape(KC, 128, NE))
    bgate_t = np.ascontiguousarray(bgate8.T.reshape(KC, 128, NE))

    inv_freq = (10000.0 ** (-np.arange(0, 64, 2, dtype=np.float64) / 64.0))
    freqs = np.arange(S, dtype=np.float64)[:, None] * inv_freq[None, :]
    cosv = np.cos(freqs).astype(f32)
    sinv = np.sin(freqs).astype(f32)
    costab = np.ascontiguousarray(
        cosv.reshape(32, 128, ROT).transpose(1, 0, 2).reshape(128, 32 * ROT))
    sintab = np.ascontiguousarray(
        sinv.reshape(32, 128, ROT).transpose(1, 0, 2).reshape(128, 32 * ROT))

    masks = np.zeros((4, 128, 512), dtype=bf16)
    qq = np.arange(512)[None, :]
    kk = np.arange(128)[:, None]
    for j in range(4):
        masks[j] = (qq >= kk + j * 128).astype(bf16)
    masks2 = np.concatenate(
        [np.concatenate([masks[0], masks[1]], axis=1),
         np.concatenate([masks[2], masks[3]], axis=1)], axis=0)

    return {
        "wqkvT": wqkvT, "woutT": woutT,
        "wr_in": np.ascontiguousarray(np.asarray(L0_ffn_Wr), dtype=f32),
        "wup_in": wup, "wgate_in": wgate, "wdown_in": wdown,
        "bup_in": bup_t, "bgate_in": bgate_t, "bdmat_in": bdmat,
        "costab_in": costab, "sintab_in": sintab,
        "br_in": np.ascontiguousarray(
            np.broadcast_to(np.asarray(L0_ffn_br, dtype=f32), (128, 7))),
        "gb_in": np.ascontiguousarray(
            np.broadcast_to(np.asarray(L0_ffn_gate_bias, dtype=f32), (128, 7))),
        "identb_in": np.eye(128, dtype=bf16),
        "identf_in": np.eye(128, dtype=f32),
        "onesb_in": np.ones((128, 128), dtype=bf16),
        "onesf_in": np.ones((1, 128), dtype=f32),
        "masks_in": np.ascontiguousarray(masks2.reshape(2 * 128, 1024)),
    }


def kernel(L0_attn_hidden_states, **weights):
    x = np.asarray(L0_attn_hidden_states, dtype=np.float32)
    B = x.shape[0]
    common = _prep_inputs(**weights)
    nc = build_program(debug=DEBUG)
    in_maps = []
    for b in range(B):
        m = dict(common)
        m["x_in"] = np.ascontiguousarray(x[b])
        in_maps.append(m)
    res = run_bass_kernel_spmd(nc, in_maps, core_ids=list(range(B)))
    return np.stack([np.asarray(r["x_out"]) for r in res.results], axis=0)
